# version 01
# baseline (speedup 1.0000x reference)
import numpy as np
import ml_dtypes
import concourse.bass as bass
import concourse.mybir as mybir
from concourse.bass_utils import run_bass_kernel_spmd

BFNP = ml_dtypes.bfloat16
F32 = mybir.dt.float32
BF16 = mybir.dt.bfloat16
AX = mybir.AxisListType.X
OP = mybir.AluOpType
AF = mybir.ActivationFunctionType

NEG_SHIFT = -16.0
EPS = 1e-5


class CFG:
    def __init__(self, D, S, NHQ, NHKV):
        self.D = D
        self.S = S
        self.NHQ = NHQ
        self.NHKV = NHKV
        self.GROUP_T = 512
        self.HD = 128
        self.NCH = D // 128
        self.GT = 4
        self.NG = S // 512
        self.NT = S // 128
        self.DKV = NHKV * 128
        self.NE = 4
        self.R = 16
        self.EW = self.NE * self.R + self.NE
        self.SCL = float(1.0 / np.sqrt(128.0))
        assert self.NHQ // self.NHKV == 4 and self.NHQ % 4 == 0
        assert self.D % 512 == 0 and self.S % 512 == 0


ENGS = ("tensor", "scalar", "vector", "sync", "gpsimd")


SEM_NAMES = ("T", "A", "V", "X0", "X1", "X2", "X3", "Z0", "Z1", "W0", "W1")
DMA_FAM = {"X": ("X0", "X1", "X2", "X3"), "Z": ("Z0", "Z1"), "W": ("W0", "W1")}


class Sched:
    def __init__(self):
        self.q = {k: [] for k in ENGS}
        self.cnt = {k: 0 for k in SEM_NAMES}
        self.rr = {k: 0 for k in DMA_FAM}
        self.waited = {k: {} for k in ENGS}
        self.semh = {}

    def op(self, eng, semkey, inc, waits, fn):
        self.cnt[semkey] += inc
        val = self.cnt[semkey]
        wl = {}
        for s, v in waits:
            if v and v > 0:
                wl[s] = max(wl.get(s, 0), v)
        wl = sorted(wl.items())

        def run(e, wl=wl, fn=fn, semkey=semkey, inc=inc, eng=eng):
            wd = self.waited[eng]
            for s, v in wl:
                if wd.get(s, 0) < v:
                    e.wait_ge(self.semh[s], v)
                    wd[s] = v
            ins = fn(e)
            if inc:
                ins.then_inc(self.semh[semkey], inc)

        self.q[eng].append(run)
        return val

    def T(self, waits, fn):
        return self.op("tensor", "T", 1, waits, fn)

    def A(self, waits, fn):
        return self.op("scalar", "A", 1, waits, fn)

    def V(self, waits, fn):
        return self.op("vector", "V", 1, waits, fn)

    def dma(self, eng, fam, waits, fn):
        k = DMA_FAM[fam][self.rr[fam] % len(DMA_FAM[fam])]
        self.rr[fam] += 1
        w = list(waits) + [(k, self.cnt[k])]
        v = self.op(eng, k, 16, w, fn)
        return (k, v)

    def X(self, waits, fn):
        return self.dma("sync", "X", waits, fn)

    def Z(self, waits, fn):
        return self.dma("scalar", "Z", waits, fn)

    def W(self, waits, fn):
        return self.dma("gpsimd", "W", waits, fn)

    def raw(self, eng, waits):
        wl = {}
        for s, v in waits:
            if v and v > 0:
                wl[s] = max(wl.get(s, 0), v)
        wl = sorted(wl.items())

        def run(e, wl=wl, eng=eng):
            wd = self.waited[eng]
            for s, v in wl:
                if wd.get(s, 0) < v:
                    e.wait_ge(self.semh[s], v)
                    wd[s] = v

        self.q[eng].append(run)


class Region:

    def __init__(self):
        self.users = []
        self.phase = None

    def use(self, sem, val):
        self.users.append((sem, val))

    def take(self, phase):
        if self.phase == phase:
            return []
        self.phase = phase
        u = self.users
        self.users = []
        return u


class Bank:
    def __init__(self, ap):
        self.ap = ap
        self.writer = 0
        self.readers = []


def build_program(cfg):
    c = cfg
    nc = bass.Bass("TRN2", target_bir_lowering=False)

    x_d = nc.dram_tensor("x", [c.S, c.D], F32, kind="ExternalInput").ap()
    wq_d = nc.dram_tensor("wq", [c.D, c.D], F32, kind="ExternalInput").ap()
    wk_d = nc.dram_tensor("wk", [c.D, c.DKV], F32, kind="ExternalInput").ap()
    wv_d = nc.dram_tensor("wv", [c.D, c.DKV], F32, kind="ExternalInput").ap()
    wo_d = nc.dram_tensor("wo", [c.D, c.D], F32, kind="ExternalInput").ap()
    awr_d = nc.dram_tensor("awr", [c.D, c.EW], F32, kind="ExternalInput").ap()
    bcat_d = nc.dram_tensor("bcat", [128, c.D // 2], F32, kind="ExternalInput").ap()
    cost_d = nc.dram_tensor("cost", [128, c.S], F32, kind="ExternalInput").ap()
    sint_d = nc.dram_tensor("sint", [128, c.S], F32, kind="ExternalInput").ap()
    iden_d = nc.dram_tensor("iden", [128, 128], BF16, kind="ExternalInput").ap()
    tri_d = nc.dram_tensor("tri", [128, 512], BF16, kind="ExternalInput").ap()
    out_d = nc.dram_tensor("out", [c.S, c.D], F32, kind="ExternalOutput").ap()

    sch = Sched()

    cursor = [0]
    backing = nc.sbuf_tensor("backing", [128, 212800], mybir.dt.uint8)
    backing_h = backing.__enter__()
    base_addr = nc.lookup_mloc(backing_h).addr

    def at(name, shape, dtype, off=None):
        per_part = int(np.prod(shape[1:])) * (4 if dtype == F32 else 2)
        if off is None:
            off = cursor[0]
            cursor[0] = off + ((per_part + 31) // 32) * 32
        assert off + per_part <= 212800
        return nc.alloc_sbuf_tensor_at(name, shape, dtype, offset=base_addr + off)

    def place(nbytes):
        off = cursor[0]
        cursor[0] += ((nbytes + 31) // 32) * 32
        return off

    TLB = c.D * 4
    kt = at("kt", [128, c.NHKV * c.S], BF16)
    vaug = at("vaug", [128, c.NT * c.NHKV * 129], BF16)
    arena1_off = place(2 * TLB)
    qtar_off = place(c.NHKV * 4 * c.GROUP_T * 2)
    hextra_off = place(2 * TLB)

    h = [at("h0", [128, c.D], F32, arena1_off),
         at("h1", [128, c.D], F32, arena1_off + TLB),
         at("h2", [128, c.D], F32, hextra_off),
         at("h3", [128, c.D], F32, hextra_off + TLB)]
    attn_out = [at(f"ao{j}", [128, c.D], BF16, arena1_off + j * c.D * 2)
                for j in range(c.GT)]
    xnT = at("xnT", [128, c.NCH * c.GROUP_T], BF16, arena1_off)
    qt = at("qt", [128, c.NHKV * 4 * c.GROUP_T], BF16, qtar_off)
    aoT = at("aoT", [128, c.NCH * c.GROUP_T], BF16, qtar_off)
    hnT = at("hnT", [128, c.NCH * c.GROUP_T], BF16, qtar_off)

    wblk = [at(f"wblk{i}", [128, 8 * 512], BF16) for i in range(2)]
    xnhn = at("xnhn", [128, c.D], BF16)
    qswp = [at(f"qswp{i}", [128, 512], BF16) for i in range(2)]
    t1b = [at(f"t1b{i}", [128, 512], BF16) for i in range(2)]
    ebuf = [at(f"ebuf{i}", [128, 512], BF16) for i in range(2)]
    cosg = at("cosg", [128, 512], F32)
    sing = at("sing", [128, 512], F32)
    awr = at("awr", [128, c.NCH * c.EW], BF16)
    bcat = at("bcat", [128, c.D // 2], BF16)
    iden = at("iden", [128, 128], BF16)
    tri = at("tri", [128, 512], BF16)
    midlog_bf = at("midlogbf", [c.EW, c.GROUP_T], BF16)
    midlog_sb = at("midlogsb", [128, c.GT * c.EW], F32)
    gmid = at("gmid", [128, c.NE * c.R], BF16)
    gmidT = at("gmidT", [128, c.GT * 128], BF16)
    bst = at("bst", [128, 6 * (c.D // 512)], F32)
    agg = [at(f"agg{i}", [128, 2], F32) for i in range(2)]
    lnv = [at(f"lnv{i}", [128, 1], F32) for i in range(2)]
    rstd = [at(f"rstd{i}", [128, 1], F32) for i in range(2)]
    negmr = [at(f"negmr{i}", [128, 1], F32) for i in range(2)]
    gsc = at("gsc", [128, 24], F32)
    gex = [at(f"gex{i}", [128, c.NE], F32) for i in range(2)]
    gden = at("gden", [128, 2], F32)
    arcp = at("arcp", [128, 4], F32)
    cvals = at("cvals", [128, 4], F32)

    psum = []
    for i in range(8):
        cm = nc.psum_tensor(f"pb{i}", [128, 512], F32)
        psum.append(Bank(cm.__enter__()))

    for s in SEM_NAMES:
        cm = nc.semaphore(f"sem{s}")
        sch.semh[s] = cm.__enter__()

    rg_qtar = Region()
    rg_hs = [Region() for _ in range(4)]
    rg_xnhn = Region()
    rg_wblk = [Region() for _ in range(2)]
    rg_qswp = [Region() for _ in range(2)]
    rg_ebuf = [Region() for _ in range(2)]
    rg_tab = Region()
    rg_stats = [Region() for _ in range(2)]
    rg_midlog = Region()
    rg_gmid = Region()
    rg_gmidT = Region()

    blkid = [0]

    def stream_block(dram, r0, nch, c0, cw):
        buf = blkid[0] % 2
        blkid[0] += 1
        waits = rg_wblk[buf].take(("blk", blkid[0]))
        src = dram[r0:r0 + nch * 128, c0:c0 + cw].rearrange("(a p) v -> p a v", p=128)
        dst = wblk[buf][:, 0:nch * cw].rearrange("p (a v) -> p a v", v=cw)
        wv = sch.W(waits, lambda e, dst=dst, src=src: e.dma_start(dst, src))
        return buf, wv

    def mm(bank, out_ap, lhsT, rhs, start, stop, waits):
        w = list(waits)
        if start:
            w += bank.readers
            bank.readers = []
        tv = sch.T(w, lambda e, o=out_ap, l=lhsT, r=rhs, st=start, sp=stop:
                   nc.tensor.matmul(o, l, r, start=st, stop=sp))
        if stop:
            bank.writer = tv
        return tv

    def evac_quad(bank, dstT, cq, jl, extra_waits):
        src = bank.ap[:, 0:512].rearrange("p (a q) -> p a q", q=128)
        dst = dstT[:, cq * 4 * c.GROUP_T:(cq * 4 + 4) * c.GROUP_T] \
            .rearrange("p (a t) -> p a t", t=c.GROUP_T)[:, :, jl * 128:(jl + 1) * 128]
        vv = sch.V([("T", bank.writer)] + extra_waits,
                   lambda e, d=dst, s=src: nc.vector.tensor_copy(d, s))
        bank.readers.append(("V", vv))
        return vv

    def transpose_tile(src_bf, dstT, jl, ready_fn, first_evac_waits):
        res = []
        for cq in range(c.NCH // 4):
            bank = psum[cq % 2]
            tv = 0
            for ciq in range(4):
                ci = cq * 4 + ciq
                w = ready_fn(ci)
                if ciq == 0:
                    w = w + bank.readers
                    bank.readers = []
                tv = sch.T(w, lambda e, b=bank, ciq=ciq, ci=ci, s=src_bf:
                           nc.tensor.matmul(b.ap[:, ciq * 128:(ciq + 1) * 128],
                                            s[:, ci * 128:(ci + 1) * 128], iden[:, :],
                                            start=True, stop=True))
            bank.writer = tv
            ew = first_evac_waits if cq == 0 else []
            res.append((cq, evac_quad(bank, dstT, cq, jl, ew)))
        return res

    def layer_norm(xt, jl, extra_waits, dst_bf, dst_waits):
        p = jl % 2
        wst = rg_stats[p].take(("ln", sch.cnt["V"]))
        v1 = 0
        for su in range(c.D // 512):
            v1 = sch.V(extra_waits + wst if su == 0 else [],
                       lambda e, s=xt, su=su: nc.vector.bn_stats(
                           bst[:, 6 * su:6 * (su + 1)], s[:, 512 * su:512 * (su + 1)]))
        v2 = sch.V([("V", v1)], lambda e, p=p: nc.vector.bn_aggr(agg[p][:, :], bst[:, :]))
        a1 = sch.A([("V", v2)], lambda e, p=p: nc.scalar.activation(
            lnv[p][:, :], agg[p][:, 1:2], AF.Ln, bias=cvals[:, 0:1]))
        a2 = sch.A([("A", a1)], lambda e, p=p: nc.scalar.activation(
            rstd[p][:, :], lnv[p][:, :], AF.Exp, scale=cvals[:, 1:2]))
        v3 = sch.V([("A", a2), ("V", v2)], lambda e, p=p: nc.vector.tensor_scalar(
            negmr[p][:, :], agg[p][:, 0:1], rstd[p][:, 0:1], -1.0, OP.mult, OP.mult))
        rg_stats[p].use("V", v3)
        a3 = sch.A([("V", v3), ("A", a2)] + dst_waits, lambda e, s=xt, d=dst_bf, p=p:
                   nc.scalar.activation(d[:, :], s[:, :], AF.Identity,
                                        bias=negmr[p][:, 0:1], scale=rstd[p][:, 0:1]))
        rg_stats[p].use("A", a3)
        return v1, a3

    def rope(dst_ap, swap_hi, swap_lo, cos_ap, sin_ap, sw_ap, t1_ap, sb, waits, prod):
        wq0 = rg_qswp[sb].take(("rp", sch.rr["Z"]))
        z1 = sch.Z([("A", prod)] + wq0 + waits,
                   lambda e, s=swap_hi, sb=sb: e.dma_start(qswp[sb][0:64, :], s))
        z2 = sch.Z([("A", prod)] + wq0,
                   lambda e, s=swap_lo, sb=sb: e.dma_start(qswp[sb][64:128, :], s))
        v1 = sch.V([z1, z2, ("A", prod)],
                   lambda e, t=t1_ap, sw=sw_ap, s=sin_ap:
                   nc.vector.tensor_tensor(t, sw, s, OP.mult))
        v2 = sch.V([("V", v1)], lambda e, sw=sw_ap, d=dst_ap, co=cos_ap:
                   nc.vector.tensor_tensor(sw, d, co, OP.mult))
        v3 = sch.V([("V", v2)], lambda e, d=dst_ap, t=t1_ap, sw=sw_ap:
                   nc.vector.tensor_tensor(d, t, sw, OP.add))
        rg_qswp[sb].use("V", v3)
        return v3

    def qt_head(k, g4, p0=0, p1=128):
        return qt[p0:p1, :].rearrange("p (k j q) -> p k j q", k=c.NHKV, q=512)[
            :, k:k + 1, :, g4 * 128:(g4 + 1) * 128]

    def four_dim(ap2d):
        return ap2d.rearrange("p (k j q) -> p k j q", k=1, q=128)

    xc1 = sch.X([], lambda e: e.dma_start(iden[:, :], iden_d[:, :]))
    xc2 = sch.X([], lambda e: e.dma_start(tri[:, :], tri_d[:, :]))
    asrc = awr_d[:, :].rearrange("(a p) v -> p a v", p=128)
    adst = awr[:, :].rearrange("p (a v) -> p a v", v=c.EW)
    wc1 = sch.W([], lambda e, d=adst, s=asrc: e.dma_start(d, s))
    wc2 = sch.W([], lambda e: e.dma_start(bcat[:, :], bcat_d[:, :]))
    ones_ap = vaug[:, :].rearrange("p (s v) -> p s v", v=129)[:, :, 128:129]
    vo = sch.V([], lambda e, a=ones_ap: nc.vector.memset(a, 1.0))
    sch.V([], lambda e: nc.vector.memset(cvals[:, 0:1], EPS))
    sch.V([], lambda e: nc.vector.memset(cvals[:, 1:2], -0.5))
    sch.V([], lambda e: nc.vector.memset(cvals[:, 2:3], c.SCL))
    cvr = sch.V([], lambda e: nc.vector.memset(cvals[:, 3:4], NEG_SHIFT))
    sch.raw("scalar", [("V", cvr)])
    tb = sch.X([], lambda e: e.dma_start(cosg[:, :], cost_d[:, 0:512]))
    tb = sch.X([], lambda e: e.dma_start(sing[:, :], sint_d[:, 0:512]))
    rg_tab.use(*tb)
    iden_ready = xc1
    tri_ready = xc2
    awr_ready = wc1
    bcat_ready = wc2

    kt_ready = [[0] * c.NG for _ in range(c.NHKV)]
    vaug_ready = [[0] * c.NT for _ in range(c.NHKV)]
    qsw_ready = [0] * c.NHKV

    for g in range(c.NG):
        nblk = (c.NCH + 7) // 8
        xnT_blk_ready = [0] * nblk
        axv = [0] * c.GT
        for jl in range(c.GT):
            row0 = (g * c.GT + jl) * 128
            slot = 2 + (jl % 2)
            wts = rg_hs[slot].take(("Ax", g, jl))
            xv = sch.X(wts, lambda e, r=row0, s=slot: e.dma_start(h[s][:, :],
                                                                  x_d[r:r + 128, :]))
            axv[jl] = xv
            rg_hs[slot].use(*xv)
            wxn = rg_xnhn.take(("A", g, jl))
            v1, a3 = layer_norm(h[slot], jl, [xv], xnhn, wxn)
            rg_hs[slot].use("V", v1)
            rg_hs[slot].use("A", a3)
            warena = (rg_hs[0].take(("xnT", g)) + rg_hs[1].take(("xnT", g))) if jl == 0 else []
            for cq, vv in transpose_tile(xnhn, xnT, jl, lambda ci, a3=a3: [("A", a3)],
                                         warena):
                if jl == c.GT - 1 and (cq % 2 == 1 or cq == c.NCH // 4 - 1):
                    xnT_blk_ready[min(cq // 2, nblk - 1)] = vv
            rg_xnhn.use("T", sch.cnt["T"])
        for b in range(nblk):
            if xnT_blk_ready[b] == 0:
                xnT_blk_ready[b] = xnT_blk_ready[-1]

        wtab = [tb]
        for s in range(c.NHQ // 4):
            base = (s % 2) * 4
            for b in range(nblk):
                nch = min(8, c.NCH - b * 8)
                buf, wv = stream_block(wq_d, b * 8 * 128, nch, s * 512, 512)
                for cc in range(nch):
                    ci = b * 8 + cc
                    for hh in range(4):
                        bank = psum[base + hh]
                        mm(bank, bank.ap[:, 0:512],
                           wblk[buf][:, cc * 512 + hh * 128:cc * 512 + (hh + 1) * 128],
                           xnT[:, ci * 512:(ci + 1) * 512],
                           start=(ci == 0), stop=(ci == c.NCH - 1),
                           waits=[wv, ("V", xnT_blk_ready[b])])
                rg_wblk[buf].use("T", sch.cnt["T"])
            for hh in range(4):
                bank = psum[base + hh]
                dst = qt_head(s, hh)
                wq0 = rg_qtar.take(("qt", g)) if (s == 0 and hh == 0) else []
                a1 = sch.A([("T", bank.writer)] + wq0,
                           lambda e, d=dst, b=bank: nc.scalar.activation(
                               d, b.ap[:, 0:512], AF.Identity))
                bank.readers.append(("A", a1))
                sb = hh % 2
                rope(dst, qt_head(s, hh, 64, 128), qt_head(s, hh, 0, 64),
                     four_dim(cosg[:, :]), four_dim(sing[:, :]),
                     four_dim(qswp[sb][:, :]), four_dim(t1b[sb][:, :]),
                     sb, wtab, a1)
            qsw_ready[s] = sch.cnt["V"]

        for s in range((c.NHKV + 3) // 4):
            kh = min(4, c.NHKV - s * 4)
            cw = kh * 128
            base = (s % 2) * 4
            for b in range(nblk):
                nch = min(8, c.NCH - b * 8)
                buf, wv = stream_block(wk_d, b * 8 * 128, nch, s * 512, cw)
                for cc in range(nch):
                    ci = b * 8 + cc
                    for kk in range(kh):
                        bank = psum[base + kk]
                        mm(bank, bank.ap[:, 0:512],
                           wblk[buf][:, cc * cw + kk * 128:cc * cw + (kk + 1) * 128],
                           xnT[:, ci * 512:(ci + 1) * 512],
                           start=(ci == 0), stop=(ci == c.NCH - 1),
                           waits=[wv, ("V", xnT_blk_ready[b])])
                rg_wblk[buf].use("T", sch.cnt["T"])
            for kk in range(kh):
                k = s * 4 + kk
                bank = psum[base + kk]
                co = k * c.S + g * c.GROUP_T
                dst = kt[:, co:co + 512]
                a1 = sch.A([("T", bank.writer)], lambda e, d=dst, b=bank:
                           nc.scalar.activation(d, b.ap[:, 0:512], AF.Identity))
                bank.readers.append(("A", a1))
                sb = kk % 2
                kt_ready[k][g] = rope(dst, kt[64:128, co:co + 512], kt[0:64, co:co + 512],
                                      cosg[:, :], sing[:, :],
                                      qswp[sb][:, :], t1b[sb][:, :], sb, wtab, a1)

        for s in range((c.DKV + 511) // 512):
            cw = min(512, c.DKV - s * 512)
            base = (s % 2) * 4
            for b in range(nblk):
                nch = min(8, c.NCH - b * 8)
                buf, wv = stream_block(wv_d, b * 8 * 128, nch, s * 512, cw)
                for cc in range(nch):
                    ci = b * 8 + cc
                    for jl in range(c.GT):
                        bank = psum[base + jl]
                        mm(bank, bank.ap[:, 0:cw],
                           xnT[:, ci * 512 + jl * 128:ci * 512 + (jl + 1) * 128],
                           wblk[buf][:, cc * cw:(cc + 1) * cw],
                           start=(ci == 0), stop=(ci == c.NCH - 1),
                           waits=[wv, ("V", xnT_blk_ready[b])])
                rg_wblk[buf].use("T", sch.cnt["T"])
            for jl in range(c.GT):
                i = g * c.GT + jl
                bank = psum[base + jl]
                nh = cw // 128
                dst = vaug[:, :].rearrange("p (t k v) -> p t k v", k=c.NHKV, v=129)[
                    :, i:i + 1, s * 4:s * 4 + nh, 0:128]
                src = bank.ap[:, 0:cw].rearrange("p (k v) -> p k v", v=128)
                vv = sch.V([("T", bank.writer), ("V", vo)],
                           lambda e, d=dst, s2=src: nc.vector.tensor_copy(d, s2))
                bank.readers.append(("V", vv))
                for kk in range(nh):
                    vaug_ready[s * 4 + kk][i] = vv
        b_done_T = sch.cnt["T"]
        rg_hs[0].use("T", b_done_T)
        rg_hs[1].use("T", b_done_T)

        if g + 1 < c.NG:
            wt = rg_tab.take(("tab", g + 1))
            lastrope = ("V", sch.cnt["V"])
            sch.X([lastrope] + wt, lambda e, gg=g + 1: e.dma_start(
                cosg[:, :], cost_d[:, gg * 512:(gg + 1) * 512]))
            tb = sch.X([], lambda e, gg=g + 1: e.dma_start(
                sing[:, :], sint_d[:, gg * 512:(gg + 1) * 512]))
            rg_tab.use(*tb)

        epar = 0
        pvpar = 0
        attn_ready = [[0] * c.NHKV for _ in range(c.GT)]
        for jl in range(c.GT):
            jg = g * c.GT + jl
            for k in range(c.NHKV):
                pvb = [psum[4 + 2 * pvpar], psum[5 + 2 * pvpar]]
                for i in range(jg + 1):
                    sb_ = psum[epar]
                    w = [("V", kt_ready[k][i // c.GT]), ("V", qsw_ready[k])] + sb_.readers
                    sb_.readers = []
                    tv = sch.T(w, lambda e, o=sb_.ap, k=k, i=i, jl=jl:
                               nc.tensor.matmul(
                                   o[:, 0:512],
                                   kt[:, k * c.S + i * 128:k * c.S + (i + 1) * 128],
                                   qt[:, (k * 4 + jl) * 512:(k * 4 + jl + 1) * 512],
                                   start=True, stop=True))
                    sb_.writer = tv
                    we = rg_ebuf[epar].take(("e", sch.cnt["A"]))
                    av = sch.A([("T", tv)] + we, lambda e, p=epar, b=sb_:
                               nc.scalar.activation(ebuf[p][:, :], b.ap[:, 0:512],
                                                    AF.Exp, bias=cvals[:, 3:4],
                                                    scale=cvals[:, 2:3]))
                    sb_.readers.append(("A", av))
                    eready = ("A", av)
                    if i == jg:
                        vv = sch.V([eready, tri_ready], lambda e, p=epar:
                                   nc.vector.tensor_tensor(ebuf[p][:, :], ebuf[p][:, :],
                                                           tri[:, :], OP.mult))
                        eready = ("V", vv)
                    for g4 in range(4):
                        w2 = [eready, ("V", vaug_ready[k][i])]
                        bank = pvb[g4 // 2]
                        if i == 0 and g4 == 0:
                            w2 += pvb[0].readers + pvb[1].readers
                            pvb[0].readers = []
                            pvb[1].readers = []
                        tv2 = sch.T(w2, lambda e, b=bank, p=epar, g4=g4, k=k, i=i, jg=jg:
                                    nc.tensor.matmul(
                                        b.ap[:, (g4 % 2) * 129:(g4 % 2) * 129 + 129],
                                        ebuf[p][:, g4 * 128:(g4 + 1) * 128],
                                        vaug[:, (i * c.NHKV + k) * 129:
                                             (i * c.NHKV + k) * 129 + 129],
                                        start=(i == 0 and g4 % 2 == 0),
                                        stop=(i == jg and g4 % 2 == 1)))
                        rg_ebuf[epar].use("T", tv2)
                        if i == jg:
                            bank.writer = tv2
                    epar ^= 1
                rv = 0
                for bb in range(2):
                    dn = pvb[bb].ap[:, 0:258].rearrange("p (g v) -> p g v", v=129)[
                        :, :, 128:129]
                    rv = sch.V([("T", pvb[0].writer), ("T", pvb[1].writer)],
                               lambda e, d=dn, bb=bb: nc.vector.reciprocal(
                                   arcp[:, bb * 2:bb * 2 + 2], d))
                    pvb[bb].readers.append(("V", rv))
                wa = rg_hs[jl // 2].take(("ao", g)) if k == 0 else []
                for g4 in range(4):
                    bank = pvb[g4 // 2]
                    hcol = (4 * k + g4) * 128
                    vv = sch.V([("T", bank.writer), ("V", rv)] + (wa if g4 == 0 else []),
                               lambda e, b=bank, g4=g4, jl=jl, hc=hcol:
                               nc.vector.tensor_scalar(
                                   attn_out[jl][:, hc:hc + 128],
                                   b.ap[:, (g4 % 2) * 129:(g4 % 2) * 129 + 128],
                                   arcp[:, g4:g4 + 1], 0.0, OP.mult, OP.add))
                    bank.readers.append(("V", vv))
                    rg_hs[jl // 2].use("V", vv)
                attn_ready[jl][k] = sch.cnt["V"]
                pvpar ^= 1
        rg_qtar.use("T", sch.cnt["T"])

        aoT_blk_ready = [0] * nblk
        wq0 = rg_qtar.take(("aoT", g))
        for jl in range(c.GT):
            for cq, vv in transpose_tile(
                    attn_out[jl], aoT, jl,
                    lambda ci, jl=jl: [("V", attn_ready[jl][ci // 4])],
                    wq0 if jl == 0 else []):
                if jl == c.GT - 1 and (cq % 2 == 1 or cq == c.NCH // 4 - 1):
                    aoT_blk_ready[min(cq // 2, nblk - 1)] = vv
            rg_hs[jl // 2].use("T", sch.cnt["T"])
        for b in range(nblk):
            if aoT_blk_ready[b] == 0:
                aoT_blk_ready[b] = aoT_blk_ready[-1]
        xdma = [0] * c.GT
        hw0 = [[] for _ in range(c.GT)]
        for jl in range(c.GT):
            row0 = (g * c.GT + jl) * 128
            wts = rg_hs[jl].take(("h", g))
            if jl < 2:
                xdma[jl] = sch.X(wts, lambda e, r=row0, jl=jl:
                                 e.dma_start(h[jl][:, :], x_d[r:r + 128, :]))
            else:
                xdma[jl] = axv[jl]
                hw0[jl] = wts
        h_add = [[0] * (c.D // 512) for _ in range(c.GT)]
        for s in range(c.D // 512):
            base = (s % 2) * 4
            for b in range(nblk):
                nch = min(8, c.NCH - b * 8)
                buf, wv = stream_block(wo_d, b * 8 * 128, nch, s * 512, 512)
                for cc in range(nch):
                    ci = b * 8 + cc
                    for jl in range(c.GT):
                        bank = psum[base + jl]
                        mm(bank, bank.ap[:, 0:512],
                           aoT[:, ci * 512 + jl * 128:ci * 512 + (jl + 1) * 128],
                           wblk[buf][:, cc * 512:(cc + 1) * 512],
                           start=(ci == 0), stop=(ci == c.NCH - 1),
                           waits=[wv, ("V", aoT_blk_ready[b])])
                rg_wblk[buf].use("T", sch.cnt["T"])
            for jl in range(c.GT):
                bank = psum[base + jl]
                vv = sch.V([("T", bank.writer), xdma[jl]]
                           + (hw0[jl] if s == 0 else []),
                           lambda e, b=bank, jl=jl, s=s: nc.vector.tensor_tensor(
                               h[jl][:, s * 512:(s + 1) * 512], b.ap[:, 0:512],
                               h[jl][:, s * 512:(s + 1) * 512], OP.add))
                bank.readers.append(("V", vv))
                h_add[jl][s] = vv
                rg_hs[jl].use("V", vv)
        rg_qtar.use("T", sch.cnt["T"])

        hnT_blk_ready = [0] * nblk
        wq0 = rg_qtar.take(("hnT", g))
        for jl in range(c.GT):
            wxn = rg_xnhn.take(("E", g, jl))
            v1, a3 = layer_norm(h[jl], jl, [("V", h_add[jl][-1])], xnhn, wxn)
            rg_hs[jl].use("V", v1)
            rg_hs[jl].use("A", a3)
            for cq, vv in transpose_tile(xnhn, hnT, jl, lambda ci, a3=a3: [("A", a3)],
                                         wq0 if jl == 0 else []):
                if jl == c.GT - 1 and (cq % 2 == 1 or cq == c.NCH // 4 - 1):
                    hnT_blk_ready[min(cq // 2, nblk - 1)] = vv
            rg_xnhn.use("T", sch.cnt["T"])
        for b in range(nblk):
            if hnT_blk_ready[b] == 0:
                hnT_blk_ready[b] = hnT_blk_ready[-1]
        bank_m = psum[2]
        for ci in range(c.NCH):
            mm(bank_m, bank_m.ap[0:c.EW, 0:512],
               awr[:, ci * c.EW:(ci + 1) * c.EW], hnT[:, ci * 512:(ci + 1) * 512],
               start=(ci == 0), stop=(ci == c.NCH - 1),
               waits=[awr_ready, ("V", hnT_blk_ready[ci // 8])])
        rg_qtar.use("T", sch.cnt["T"])
        wml = rg_midlog.take(("ml", g))
        aml = sch.A([("T", bank_m.writer)] + wml, lambda e: nc.scalar.activation(
            midlog_bf[:, :], bank_m.ap[0:c.EW, 0:512], AF.Identity))
        bank_m.readers.append(("A", aml))
        bank_t = psum[3]
        for jl in range(c.GT):
            w = [("A", aml), iden_ready] + (bank_t.readers if jl == 0 else [])
            if jl == 0:
                bank_t.readers = []
            tv = sch.T(w, lambda e, jl=jl: nc.tensor.matmul(
                bank_t.ap[:, jl * c.EW:(jl + 1) * c.EW],
                midlog_bf[:, jl * 128:(jl + 1) * 128],
                iden[0:c.EW, 0:c.EW], start=True, stop=True))
            rg_midlog.use("T", tv)
            bank_t.writer = tv
        vml = sch.V([("T", bank_t.writer)], lambda e: nc.vector.tensor_copy(
            midlog_sb[:, 0:c.GT * c.EW], bank_t.ap[:, 0:c.GT * c.EW]))
        bank_t.readers.append(("V", vml))

        bank_gt = psum[4]
        for jl in range(c.GT):
            lg = midlog_sb[:, jl * c.EW + c.NE * c.R:(jl + 1) * c.EW]
            md = midlog_sb[:, jl * c.EW:jl * c.EW + c.NE * c.R]
            m1, m2 = gsc[:, 0:1], gsc[:, 1:2]
            eq, l2 = gsc[:, 4:4 + c.NE], gsc[:, 8:8 + c.NE]
            mask, em = gsc[:, 12:12 + c.NE], gsc[:, 16:16 + c.NE]
            g4v = gsc[:, 20:20 + c.NE]
            p = jl % 2
            c1 = sch.V([("V", vml)], lambda e, lg=lg, m1=m1:
                       nc.vector.reduce_max(m1, lg, AX, negate=True))
            c2 = sch.V([("V", c1)], lambda e, lg=lg, m1=m1, eq=eq:
                       nc.vector.tensor_scalar(
                           eq, lg, m1[:, 0:1], 0.0, OP.add, OP.is_equal))
            c3 = sch.V([("V", c2)], lambda e, lg=lg, eq=eq, l2=l2:
                       nc.vector.scalar_tensor_tensor(
                           l2, eq, -1e30, lg, OP.mult, OP.add))
            c4 = sch.V([("V", c3)], lambda e, l2=l2, m2=m2:
                       nc.vector.reduce_max(m2, l2, AX, negate=True))
            vmask = sch.V([("V", c4)], lambda e, lg=lg, m2=m2, mask=mask:
                          nc.vector.tensor_scalar(
                              mask, lg, m2[:, 0:1], 0.0, OP.add, OP.is_ge))
            aex = sch.A([("V", vmask)], lambda e, lg=lg, m1=m1, p=p:
                        nc.scalar.activation(gex[p][:, :], lg, AF.Exp, bias=m1[:, 0:1]))
            c5 = sch.V([("A", aex), ("V", vmask)], lambda e, em=em, p=p, mask=mask:
                       nc.vector.tensor_tensor(em, gex[p][:, :], mask, OP.mult))
            c6 = sch.V([("V", c5)], lambda e, em=em:
                       nc.vector.reduce_sum(gden[:, 0:1], em, AX))
            c7 = sch.V([("V", c6)], lambda e:
                       nc.vector.reciprocal(gden[:, 1:2], gden[:, 0:1]))
            vg = sch.V([("V", c7)], lambda e, em=em, g4v=g4v: nc.vector.tensor_scalar(
                g4v, em, gden[:, 1:2], 4.0, OP.mult, OP.mult))
            wgm = rg_gmid.take(("gm", g, jl))
            vgm = 0
            for e4 in range(c.NE):
                vgm = sch.V([("V", vg)] + (wgm if e4 == 0 else []),
                            lambda e, md=md, e4=e4, g4v=g4v: nc.vector.tensor_scalar(
                                gmid[:, e4 * c.R:(e4 + 1) * c.R],
                                md[:, e4 * c.R:(e4 + 1) * c.R],
                                g4v[:, e4:e4 + 1], 0.0, OP.mult, OP.add))
            w = [("V", vgm), iden_ready] + (bank_gt.readers if jl == 0 else [])
            if jl == 0:
                bank_gt.readers = []
            sch.T(w, lambda e, jl=jl: nc.tensor.matmul(
                bank_gt.ap[0:c.NE * c.R, jl * 128:(jl + 1) * 128],
                gmid[:, :], iden[:, :], start=True, stop=True))
            tv = sch.T([], lambda e, jl=jl: nc.tensor.matmul(
                bank_gt.ap[c.NE * c.R:2 * c.NE * c.R, jl * 128:(jl + 1) * 128],
                gmid[:, :], iden[:, :], start=True, stop=True))
            rg_gmid.use("T", tv)
            bank_gt.writer = tv
        wgt = rg_gmidT.take(("gt", g))
        vgt = sch.V([("T", bank_gt.writer)] + wgt, lambda e: nc.vector.tensor_copy(
            gmidT[:, 0:c.GT * 128], bank_gt.ap[:, 0:c.GT * 128]))
        bank_gt.readers.append(("V", vgt))

        for jl in range(c.GT):
            last_add = 0
            for sl in range(c.D // 512):
                bank = psum[5 + sl % 2]
                w = [("V", vgt), bcat_ready] + bank.readers
                bank.readers = []
                if sl < c.D // 1024:
                    rhs = bcat[0:64, sl * 512:(sl + 1) * 512]
                    lhs = gmidT[0:64, jl * 128:(jl + 1) * 128]
                else:
                    rhs = bcat[64:128, (sl - c.D // 1024) * 512:
                               (sl - c.D // 1024 + 1) * 512]
                    lhs = gmidT[64:128, jl * 128:(jl + 1) * 128]
                tv = sch.T(w, lambda e, b=bank, l=lhs, r=rhs: nc.tensor.matmul(
                    b.ap[:, 0:512], l, r, start=True, stop=True))
                bank.writer = tv
                rg_gmidT.use("T", tv)
                vv = sch.V([("T", tv), ("V", h_add[jl][sl])],
                           lambda e, b=bank, jl=jl, sl=sl:
                           nc.vector.tensor_tensor(
                               h[jl][:, sl * 512:(sl + 1) * 512], b.ap[:, 0:512],
                               h[jl][:, sl * 512:(sl + 1) * 512], OP.add))
                bank.readers.append(("V", vv))
                last_add = vv
                rg_hs[jl].use("V", vv)
            row0 = (g * c.GT + jl) * 128
            xv = sch.X([("V", last_add)], lambda e, r=row0, jl=jl:
                       e.dma_start(out_d[r:r + 128, :], h[jl][:, :]))
            rg_hs[jl].use(*xv)

    sch.raw("sync", [(k, sch.cnt[k]) for k in DMA_FAM["X"]])
    sch.raw("gpsimd", [(k, sch.cnt[k]) for k in DMA_FAM["W"]])
    sch.raw("scalar", [(k, sch.cnt[k]) for k in DMA_FAM["Z"]])

    with nc.Block() as block:
        @block.tensor
        def _(e):
            for f in sch.q["tensor"]:
                f(e)

        @block.scalar
        def _(e):
            for f in sch.q["scalar"]:
                f(e)

        @block.vector
        def _(e):
            for f in sch.q["vector"]:
                f(e)

        @block.sync
        def _(e):
            for f in sch.q["sync"]:
                f(e)

        @block.gpsimd
        def _(e):
            for f in sch.q["gpsimd"]:
                f(e)

    return nc


def make_tables(cfg):
    c = cfg
    ex = np.arange(0, 128, 2, dtype=np.float32) / np.float32(128.0)
    inv = (np.float32(1.0) / (np.float32(10000.0) ** ex)).astype(np.float32)
    t = np.arange(c.S, dtype=np.float32)
    fr = t[:, None] * inv[None, :]
    cosv = np.cos(fr).astype(np.float32)
    sinv = np.sin(fr).astype(np.float32)
    cost = np.concatenate([cosv.T, cosv.T], axis=0)
    sint = np.concatenate([-sinv.T, sinv.T], axis=0)
    iden = np.eye(128, dtype=np.float32).astype(BFNP)
    m = np.arange(128)[:, None]
    q = np.arange(128)[None, :]
    trib = (q >= m).astype(np.float32)
    tri = np.ascontiguousarray(np.concatenate([trib] * 4, axis=1)).astype(BFNP)
    return cost, sint, iden, tri


def make_inputs(cfg, x, wq, wk, wv, wo, A, B, Wr):
    c = cfg
    cost, sint, iden, tri = make_tables(c)
    awr = np.ascontiguousarray(np.concatenate(
        [np.ascontiguousarray(A.transpose(2, 0, 1)).reshape(c.D, c.NE * c.R),
         np.ascontiguousarray(Wr.T)], axis=1), dtype=np.float32)
    bc = np.ascontiguousarray(B.transpose(0, 2, 1)).reshape(c.NE * c.R, c.D)
    bcat = np.ascontiguousarray(
        np.concatenate([bc[:, :c.D // 2], bc[:, c.D // 2:]], axis=0), dtype=np.float32)
    return dict(x=np.ascontiguousarray(x, dtype=np.float32),
                wq=np.ascontiguousarray(wq, dtype=np.float32),
                wk=np.ascontiguousarray(wk, dtype=np.float32),
                wv=np.ascontiguousarray(wv, dtype=np.float32),
                wo=np.ascontiguousarray(wo, dtype=np.float32),
                awr=awr, bcat=bcat, cost=cost, sint=sint, iden=iden, tri=tri)


def kernel(L0_attn_x, L0_attn_Wq, L0_attn_Wk, L0_attn_Wv, L0_attn_Wo,
           L0_ffn_A, L0_ffn_B, L0_ffn_Wr):
    cfg = CFG(D=4096, S=2048, NHQ=32, NHKV=8)
    x = np.asarray(L0_attn_x, dtype=np.float32)
    nb = x.shape[0]
    base = make_inputs(cfg, x[0],
                       np.asarray(L0_attn_Wq), np.asarray(L0_attn_Wk),
                       np.asarray(L0_attn_Wv), np.asarray(L0_attn_Wo),
                       np.asarray(L0_ffn_A), np.asarray(L0_ffn_B),
                       np.asarray(L0_ffn_Wr))
    in_maps = []
    for i in range(nb):
        m = dict(base)
        m["x"] = np.ascontiguousarray(x[i])
        in_maps.append(m)
    nc = build_program(cfg)
    res = run_bass_kernel_spmd(nc, in_maps, core_ids=list(range(nb)))
    out = np.stack([np.asarray(res.results[i]["out"]) for i in range(nb)], axis=0)
    return out.astype(np.float32)


# version 02
# speedup vs baseline: 1.2580x; 1.2580x over previous
import numpy as np
import ml_dtypes
import concourse.bass as bass
import concourse.mybir as mybir
from concourse.bass_utils import run_bass_kernel_spmd

BFNP = ml_dtypes.bfloat16
F32 = mybir.dt.float32
BF16 = mybir.dt.bfloat16
AX = mybir.AxisListType.X
OP = mybir.AluOpType
AF = mybir.ActivationFunctionType

NEG_SHIFT = -16.0
EPS = 1e-5


class CFG:
    def __init__(self, D, S, NHQ, NHKV):
        self.D = D
        self.S = S
        self.NHQ = NHQ
        self.NHKV = NHKV
        self.GROUP_T = 512
        self.HD = 128
        self.NCH = D // 128
        self.GT = 4
        self.NG = S // 512
        self.NT = S // 128
        self.DKV = NHKV * 128
        self.NE = 4
        self.R = 16
        self.EW = self.NE * self.R + self.NE
        self.SCL = float(1.0 / np.sqrt(128.0))
        assert self.NHQ // self.NHKV == 4 and self.NHQ % 4 == 0
        assert self.D % 512 == 0 and self.S % 512 == 0


ENGS = ("tensor", "scalar", "vector", "sync", "gpsimd")


SEM_NAMES = ("T", "A", "V", "X0", "X1", "X2", "X3", "Z0", "Z1", "W0", "W1")
DMA_FAM = {"X": ("X0", "X1", "X2", "X3"), "Z": ("Z0", "Z1"), "W": ("W0", "W1")}


class Sched:
    def __init__(self):
        self.q = {k: [] for k in ENGS}
        self.cnt = {k: 0 for k in SEM_NAMES}
        self.rr = {k: 0 for k in DMA_FAM}
        self.waited = {k: {} for k in ENGS}
        self.semh = {}

    def op(self, eng, semkey, inc, waits, fn):
        self.cnt[semkey] += inc
        val = self.cnt[semkey]
        wl = {}
        for s, v in waits:
            if v and v > 0:
                wl[s] = max(wl.get(s, 0), v)
        wl = sorted(wl.items())

        def run(e, wl=wl, fn=fn, semkey=semkey, inc=inc, eng=eng):
            wd = self.waited[eng]
            for s, v in wl:
                if wd.get(s, 0) < v:
                    e.wait_ge(self.semh[s], v)
                    wd[s] = v
            ins = fn(e)
            if inc:
                ins.then_inc(self.semh[semkey], inc)

        self.q[eng].append(run)
        return val

    def T(self, waits, fn):
        return self.op("tensor", "T", 1, waits, fn)

    def A(self, waits, fn):
        return self.op("scalar", "A", 1, waits, fn)

    def V(self, waits, fn):
        return self.op("vector", "V", 1, waits, fn)

    def dma(self, eng, fam, waits, fn):
        k = DMA_FAM[fam][self.rr[fam] % len(DMA_FAM[fam])]
        self.rr[fam] += 1
        w = list(waits) + [(k, self.cnt[k])]
        v = self.op(eng, k, 16, w, fn)
        return (k, v)

    def X(self, waits, fn):
        return self.dma("sync", "X", waits, fn)

    def Z(self, waits, fn):
        return self.dma("scalar", "Z", waits, fn)

    def W(self, waits, fn):
        return self.dma("gpsimd", "W", waits, fn)

    def raw(self, eng, waits):
        wl = {}
        for s, v in waits:
            if v and v > 0:
                wl[s] = max(wl.get(s, 0), v)
        wl = sorted(wl.items())

        def run(e, wl=wl, eng=eng):
            wd = self.waited[eng]
            for s, v in wl:
                if wd.get(s, 0) < v:
                    e.wait_ge(self.semh[s], v)
                    wd[s] = v

        self.q[eng].append(run)


class Region:

    def __init__(self):
        self.users = []
        self.phase = None

    def use(self, sem, val):
        self.users.append((sem, val))

    def take(self, phase):
        if self.phase == phase:
            return []
        self.phase = phase
        u = self.users
        self.users = []
        return u


class Bank:
    def __init__(self, ap):
        self.ap = ap
        self.writer = 0
        self.readers = []


def build_program(cfg):
    c = cfg
    nc = bass.Bass("TRN2", target_bir_lowering=False)

    x_d = nc.dram_tensor("x", [c.S, c.D], F32, kind="ExternalInput").ap()
    wq_d = nc.dram_tensor("wq", [c.D, c.D], F32, kind="ExternalInput").ap()
    wk_d = nc.dram_tensor("wk", [c.D, c.DKV], F32, kind="ExternalInput").ap()
    wv_d = nc.dram_tensor("wv", [c.D, c.DKV], F32, kind="ExternalInput").ap()
    wo_d = nc.dram_tensor("wo", [c.D, c.D], F32, kind="ExternalInput").ap()
    awr_d = nc.dram_tensor("awr", [c.D, c.EW], F32, kind="ExternalInput").ap()
    bcat_d = nc.dram_tensor("bcat", [128, c.D // 2], F32, kind="ExternalInput").ap()
    cost_d = nc.dram_tensor("cost", [128, c.S], F32, kind="ExternalInput").ap()
    sint_d = nc.dram_tensor("sint", [128, c.S], F32, kind="ExternalInput").ap()
    iden_d = nc.dram_tensor("iden", [128, 128], BF16, kind="ExternalInput").ap()
    tri_d = nc.dram_tensor("tri", [128, 512], BF16, kind="ExternalInput").ap()
    out_d = nc.dram_tensor("out", [c.S, c.D], F32, kind="ExternalOutput").ap()

    sch = Sched()

    cursor = [0]
    backing = nc.sbuf_tensor("backing", [128, 212800], mybir.dt.uint8)
    backing_h = backing.__enter__()
    base_addr = nc.lookup_mloc(backing_h).addr

    def at(name, shape, dtype, off=None):
        per_part = int(np.prod(shape[1:])) * (4 if dtype == F32 else 2)
        if off is None:
            off = cursor[0]
            cursor[0] = off + ((per_part + 31) // 32) * 32
        assert off + per_part <= 212800
        return nc.alloc_sbuf_tensor_at(name, shape, dtype, offset=base_addr + off)

    def place(nbytes):
        off = cursor[0]
        cursor[0] += ((nbytes + 31) // 32) * 32
        return off

    TLB = c.D * 4
    kt = at("kt", [128, c.NHKV * c.S], BF16)
    vaug = at("vaug", [128, c.NT * c.NHKV * 129], BF16)
    arena1_off = place(2 * TLB)
    qtar_off = place(c.NHKV * 4 * c.GROUP_T * 2)
    hextra_off = place(2 * TLB)

    h = [at("h0", [128, c.D], F32, arena1_off),
         at("h1", [128, c.D], F32, arena1_off + TLB),
         at("h2", [128, c.D], F32, hextra_off),
         at("h3", [128, c.D], F32, hextra_off + TLB)]
    attn_out = [at(f"ao{j}", [128, c.D], BF16, arena1_off + j * c.D * 2)
                for j in range(c.GT)]
    xnT = at("xnT", [128, c.NCH * c.GROUP_T], BF16, arena1_off)
    qt = at("qt", [128, c.NHKV * 4 * c.GROUP_T], BF16, qtar_off)
    aoT = at("aoT", [128, c.NCH * c.GROUP_T], BF16, qtar_off)
    hnT = at("hnT", [128, c.NCH * c.GROUP_T], BF16, qtar_off)

    wblk = [at(f"wblk{i}", [128, 8 * 512], BF16) for i in range(2)]
    xnhn = at("xnhn", [128, c.D], BF16)
    qswp = [at(f"qswp{i}", [128, 512], BF16) for i in range(2)]
    t1b = [at(f"t1b{i}", [128, 512], BF16) for i in range(2)]
    ebuf = [at(f"ebuf{i}", [128, 512], BF16) for i in range(2)]
    cosg = at("cosg", [128, 512], F32)
    sing = at("sing", [128, 512], F32)
    awr = at("awr", [128, c.NCH * c.EW], BF16)
    bcat = at("bcat", [128, c.D // 2], BF16)
    iden = at("iden", [128, 128], BF16)
    tri = at("tri", [128, 512], BF16)
    midlog_bf = at("midlogbf", [c.EW, c.GROUP_T], BF16)
    midlog_sb = at("midlogsb", [128, c.GT * c.EW], F32)
    gmid = at("gmid", [128, c.NE * c.R], BF16)
    gmidT = at("gmidT", [128, c.GT * 128], BF16)
    bst = at("bst", [128, 6 * (c.D // 512)], F32)
    agg = [at(f"agg{i}", [128, 2], F32) for i in range(2)]
    lnv = [at(f"lnv{i}", [128, 1], F32) for i in range(2)]
    rstd = [at(f"rstd{i}", [128, 1], F32) for i in range(2)]
    negmr = [at(f"negmr{i}", [128, 1], F32) for i in range(2)]
    gsc = at("gsc", [128, 24], F32)
    gex = [at(f"gex{i}", [128, c.NE], F32) for i in range(2)]
    gden = at("gden", [128, 2], F32)
    arcp = at("arcp", [128, 4], F32)
    cvals = at("cvals", [128, 4], F32)

    psum = []
    for i in range(8):
        cm = nc.psum_tensor(f"pb{i}", [128, 512], F32)
        psum.append(Bank(cm.__enter__()))

    for s in SEM_NAMES:
        cm = nc.semaphore(f"sem{s}")
        sch.semh[s] = cm.__enter__()

    rg_qtar = Region()
    rg_hs = [Region() for _ in range(4)]
    rg_xnhn = Region()
    rg_wblk = [Region() for _ in range(2)]
    rg_qswp = [Region() for _ in range(2)]
    rg_ebuf = [Region() for _ in range(2)]
    rg_tab = Region()
    rg_stats = [Region() for _ in range(2)]
    rg_midlog = Region()
    rg_gmid = Region()
    rg_gmidT = Region()

    blkid = [0]

    def stream_block(dram, r0, nch, c0, cw):
        buf = blkid[0] % 2
        blkid[0] += 1
        waits = rg_wblk[buf].take(("blk", blkid[0]))
        src = dram[r0:r0 + nch * 128, c0:c0 + cw].rearrange("(a p) v -> p a v", p=128)
        dst = wblk[buf][:, 0:nch * cw].rearrange("p (a v) -> p a v", v=cw)
        wv = sch.W(waits, lambda e, dst=dst, src=src: e.dma_start(dst, src))
        return buf, wv

    def mm(bank, out_ap, lhsT, rhs, start, stop, waits):
        w = list(waits)
        if start:
            w += bank.readers
            bank.readers = []
        tv = sch.T(w, lambda e, o=out_ap, l=lhsT, r=rhs, st=start, sp=stop:
                   nc.tensor.matmul(o, l, r, start=st, stop=sp))
        if stop:
            bank.writer = tv
        return tv

    def evac_quad(bank, dstT, cq, jl, extra_waits):
        src = bank.ap[:, 0:512].rearrange("p (a q) -> p a q", q=128)
        dst = dstT[:, cq * 4 * c.GROUP_T:(cq * 4 + 4) * c.GROUP_T] \
            .rearrange("p (a t) -> p a t", t=c.GROUP_T)[:, :, jl * 128:(jl + 1) * 128]
        vv = sch.V([("T", bank.writer)] + extra_waits,
                   lambda e, d=dst, s=src: nc.vector.tensor_copy(d, s))
        bank.readers.append(("V", vv))
        return vv

    def transpose_tile(src_bf, dstT, jl, ready_fn, first_evac_waits):
        res = []
        for cq in range(c.NCH // 4):
            bank = psum[cq % 2]
            tv = 0
            for ciq in range(4):
                ci = cq * 4 + ciq
                w = ready_fn(ci)
                if ciq == 0:
                    w = w + bank.readers
                    bank.readers = []
                tv = sch.T(w, lambda e, b=bank, ciq=ciq, ci=ci, s=src_bf:
                           nc.tensor.matmul(b.ap[:, ciq * 128:(ciq + 1) * 128],
                                            s[:, ci * 128:(ci + 1) * 128], iden[:, :],
                                            start=True, stop=True))
            bank.writer = tv
            ew = first_evac_waits if cq == 0 else []
            res.append((cq, evac_quad(bank, dstT, cq, jl, ew)))
        return res

    def layer_norm(xt, jl, extra_waits, dst_bf, dst_waits):
        p = jl % 2
        wst = rg_stats[p].take(("ln", sch.cnt["V"]))
        v1 = 0
        for su in range(c.D // 512):
            v1 = sch.V(extra_waits + wst if su == 0 else [],
                       lambda e, s=xt, su=su: nc.vector.bn_stats(
                           bst[:, 6 * su:6 * (su + 1)], s[:, 512 * su:512 * (su + 1)]))
        v2 = sch.V([("V", v1)], lambda e, p=p: nc.vector.bn_aggr(agg[p][:, :], bst[:, :]))
        a1 = sch.A([("V", v2)], lambda e, p=p: nc.scalar.activation(
            lnv[p][:, :], agg[p][:, 1:2], AF.Ln, bias=cvals[:, 0:1]))
        a2 = sch.A([("A", a1)], lambda e, p=p: nc.scalar.activation(
            rstd[p][:, :], lnv[p][:, :], AF.Exp, scale=cvals[:, 1:2]))
        v3 = sch.V([("A", a2), ("V", v2)], lambda e, p=p: nc.vector.tensor_scalar(
            negmr[p][:, :], agg[p][:, 0:1], rstd[p][:, 0:1], -1.0, OP.mult, OP.mult))
        rg_stats[p].use("V", v3)
        a3 = sch.A([("V", v3), ("A", a2)] + dst_waits, lambda e, s=xt, d=dst_bf, p=p:
                   nc.scalar.activation(d[:, :], s[:, :], AF.Identity,
                                        bias=negmr[p][:, 0:1], scale=rstd[p][:, 0:1]))
        rg_stats[p].use("A", a3)
        return v1, a3

    def rope(dst_ap, swap_hi, swap_lo, cos_ap, sin_ap, sw_ap, t1_ap, sb, waits, prod):
        wq0 = rg_qswp[sb].take(("rp", sch.rr["Z"]))
        z1 = sch.Z([("A", prod)] + wq0 + waits,
                   lambda e, s=swap_hi, sb=sb: e.dma_start(qswp[sb][0:64, :], s))
        z2 = sch.Z([("A", prod)] + wq0,
                   lambda e, s=swap_lo, sb=sb: e.dma_start(qswp[sb][64:128, :], s))
        v1 = sch.V([z1, z2, ("A", prod)],
                   lambda e, t=t1_ap, sw=sw_ap, s=sin_ap:
                   nc.vector.tensor_tensor(t, sw, s, OP.mult))
        v2 = sch.V([("V", v1)], lambda e, sw=sw_ap, d=dst_ap, co=cos_ap:
                   nc.vector.tensor_tensor(sw, d, co, OP.mult))
        v3 = sch.V([("V", v2)], lambda e, d=dst_ap, t=t1_ap, sw=sw_ap:
                   nc.vector.tensor_tensor(d, t, sw, OP.add))
        rg_qswp[sb].use("V", v3)
        return v3

    def qt_head(k, g4, p0=0, p1=128):
        return qt[p0:p1, :].rearrange("p (k j q) -> p k j q", k=c.NHKV, q=512)[
            :, k:k + 1, :, g4 * 128:(g4 + 1) * 128]

    def four_dim(ap2d):
        return ap2d.rearrange("p (k j q) -> p k j q", k=1, q=128)

    xc1 = sch.X([], lambda e: e.dma_start(iden[:, :], iden_d[:, :]))
    xc2 = sch.X([], lambda e: e.dma_start(tri[:, :], tri_d[:, :]))
    asrc = awr_d[:, :].rearrange("(a p) v -> p a v", p=128)
    adst = awr[:, :].rearrange("p (a v) -> p a v", v=c.EW)
    wc1 = sch.W([], lambda e, d=adst, s=asrc: e.dma_start(d, s))
    wc2 = sch.W([], lambda e: e.dma_start(bcat[:, :], bcat_d[:, :]))
    ones_ap = vaug[:, :].rearrange("p (s v) -> p s v", v=129)[:, :, 128:129]
    vo = sch.V([], lambda e, a=ones_ap: nc.vector.memset(a, 1.0))
    sch.V([], lambda e: nc.vector.memset(cvals[:, 0:1], EPS))
    sch.V([], lambda e: nc.vector.memset(cvals[:, 1:2], -0.5))
    sch.V([], lambda e: nc.vector.memset(cvals[:, 2:3], c.SCL))
    cvr = sch.V([], lambda e: nc.vector.memset(cvals[:, 3:4], NEG_SHIFT))
    sch.raw("scalar", [("V", cvr)])
    tb = sch.X([], lambda e: e.dma_start(cosg[:, :], cost_d[:, 0:512]))
    tb = sch.X([], lambda e: e.dma_start(sing[:, :], sint_d[:, 0:512]))
    rg_tab.use(*tb)
    iden_ready = xc1
    tri_ready = xc2
    awr_ready = wc1
    bcat_ready = wc2

    kt_ready = [[0] * c.NG for _ in range(c.NHKV)]
    vaug_ready = [[0] * c.NT for _ in range(c.NHKV)]
    qsw_ready = [0] * c.NHKV

    for g in range(c.NG):
        nblk = (c.NCH + 7) // 8
        xnT_blk_ready = [0] * nblk
        axv = [0] * c.GT
        for jl in range(c.GT):
            row0 = (g * c.GT + jl) * 128
            slot = 2 + (jl % 2)
            wts = rg_hs[slot].take(("Ax", g, jl))
            xv = sch.X(wts, lambda e, r=row0, s=slot: e.dma_start(h[s][:, :],
                                                                  x_d[r:r + 128, :]))
            axv[jl] = xv
            rg_hs[slot].use(*xv)
            wxn = rg_xnhn.take(("A", g, jl))
            v1, a3 = layer_norm(h[slot], jl, [xv], xnhn, wxn)
            rg_hs[slot].use("V", v1)
            rg_hs[slot].use("A", a3)
            warena = (rg_hs[0].take(("xnT", g)) + rg_hs[1].take(("xnT", g))) if jl == 0 else []
            for cq, vv in transpose_tile(xnhn, xnT, jl, lambda ci, a3=a3: [("A", a3)],
                                         warena):
                if jl == c.GT - 1 and (cq % 2 == 1 or cq == c.NCH // 4 - 1):
                    xnT_blk_ready[min(cq // 2, nblk - 1)] = vv
            rg_xnhn.use("T", sch.cnt["T"])
        for b in range(nblk):
            if xnT_blk_ready[b] == 0:
                xnT_blk_ready[b] = xnT_blk_ready[-1]

        wtab = [tb]
        for s in range(c.NHQ // 4):
            base = (s % 2) * 4
            for b in range(nblk):
                nch = min(8, c.NCH - b * 8)
                buf, wv = stream_block(wq_d, b * 8 * 128, nch, s * 512, 512)
                for cc in range(nch):
                    ci = b * 8 + cc
                    for hh in range(4):
                        bank = psum[base + hh]
                        mm(bank, bank.ap[:, 0:512],
                           wblk[buf][:, cc * 512 + hh * 128:cc * 512 + (hh + 1) * 128],
                           xnT[:, ci * 512:(ci + 1) * 512],
                           start=(ci == 0), stop=(ci == c.NCH - 1),
                           waits=[wv, ("V", xnT_blk_ready[b])])
                rg_wblk[buf].use("T", sch.cnt["T"])
            for hh in range(4):
                bank = psum[base + hh]
                dst = qt_head(s, hh)
                wq0 = rg_qtar.take(("qt", g)) if (s == 0 and hh == 0) else []
                a1 = sch.A([("T", bank.writer)] + wq0,
                           lambda e, d=dst, b=bank: nc.scalar.activation(
                               d, b.ap[:, 0:512], AF.Identity))
                bank.readers.append(("A", a1))
                sb = hh % 2
                rope(dst, qt_head(s, hh, 64, 128), qt_head(s, hh, 0, 64),
                     four_dim(cosg[:, :]), four_dim(sing[:, :]),
                     four_dim(qswp[sb][:, :]), four_dim(t1b[sb][:, :]),
                     sb, wtab, a1)
            qsw_ready[s] = sch.cnt["V"]

        for s in range((c.NHKV + 3) // 4):
            kh = min(4, c.NHKV - s * 4)
            cw = kh * 128
            base = (s % 2) * 4
            for b in range(nblk):
                nch = min(8, c.NCH - b * 8)
                buf, wv = stream_block(wk_d, b * 8 * 128, nch, s * 512, cw)
                for cc in range(nch):
                    ci = b * 8 + cc
                    for kk in range(kh):
                        bank = psum[base + kk]
                        mm(bank, bank.ap[:, 0:512],
                           wblk[buf][:, cc * cw + kk * 128:cc * cw + (kk + 1) * 128],
                           xnT[:, ci * 512:(ci + 1) * 512],
                           start=(ci == 0), stop=(ci == c.NCH - 1),
                           waits=[wv, ("V", xnT_blk_ready[b])])
                rg_wblk[buf].use("T", sch.cnt["T"])
            for kk in range(kh):
                k = s * 4 + kk
                bank = psum[base + kk]
                co = k * c.S + g * c.GROUP_T
                dst = kt[:, co:co + 512]
                a1 = sch.A([("T", bank.writer)], lambda e, d=dst, b=bank:
                           nc.scalar.activation(d, b.ap[:, 0:512], AF.Identity))
                bank.readers.append(("A", a1))
                sb = kk % 2
                kt_ready[k][g] = rope(dst, kt[64:128, co:co + 512], kt[0:64, co:co + 512],
                                      cosg[:, :], sing[:, :],
                                      qswp[sb][:, :], t1b[sb][:, :], sb, wtab, a1)

        for s in range((c.DKV + 511) // 512):
            cw = min(512, c.DKV - s * 512)
            base = (s % 2) * 4
            for b in range(nblk):
                nch = min(8, c.NCH - b * 8)
                buf, wv = stream_block(wv_d, b * 8 * 128, nch, s * 512, cw)
                for cc in range(nch):
                    ci = b * 8 + cc
                    for jl in range(c.GT):
                        bank = psum[base + jl]
                        mm(bank, bank.ap[:, 0:cw],
                           xnT[:, ci * 512 + jl * 128:ci * 512 + (jl + 1) * 128],
                           wblk[buf][:, cc * cw:(cc + 1) * cw],
                           start=(ci == 0), stop=(ci == c.NCH - 1),
                           waits=[wv, ("V", xnT_blk_ready[b])])
                rg_wblk[buf].use("T", sch.cnt["T"])
            for jl in range(c.GT):
                i = g * c.GT + jl
                bank = psum[base + jl]
                nh = cw // 128
                dst = vaug[:, :].rearrange("p (t k v) -> p t k v", k=c.NHKV, v=129)[
                    :, i:i + 1, s * 4:s * 4 + nh, 0:128]
                src = bank.ap[:, 0:cw].rearrange("p (k v) -> p k v", v=128)
                vv = sch.V([("T", bank.writer), ("V", vo)],
                           lambda e, d=dst, s2=src: nc.vector.tensor_copy(d, s2))
                bank.readers.append(("V", vv))
                for kk in range(nh):
                    vaug_ready[s * 4 + kk][i] = vv
        b_done_T = sch.cnt["T"]
        rg_hs[0].use("T", b_done_T)
        rg_hs[1].use("T", b_done_T)

        if g + 1 < c.NG:
            wt = rg_tab.take(("tab", g + 1))
            lastrope = ("V", sch.cnt["V"])
            sch.X([lastrope] + wt, lambda e, gg=g + 1: e.dma_start(
                cosg[:, :], cost_d[:, gg * 512:(gg + 1) * 512]))
            tb = sch.X([], lambda e, gg=g + 1: e.dma_start(
                sing[:, :], sint_d[:, gg * 512:(gg + 1) * 512]))
            rg_tab.use(*tb)

        steps = []
        for jl in range(c.GT):
            jg = g * c.GT + jl
            for k in range(c.NHKV):
                for i in range(jg + 1):
                    steps.append((jl, k, i, jg))
        attn_ready = [[0] * c.NHKV for _ in range(c.GT)]

        def emit_scores(idx):
            jl, k, i, jg = steps[idx]
            sb_ = psum[idx % 2]
            w = [("V", kt_ready[k][i // c.GT]), ("V", qsw_ready[k])] + sb_.readers
            sb_.readers = []
            sb_.writer = sch.T(w, lambda e, o=sb_.ap, k=k, i=i, jl=jl:
                               nc.tensor.matmul(
                                   o[:, 0:512],
                                   kt[:, k * c.S + i * 128:k * c.S + (i + 1) * 128],
                                   qt[:, (k * 4 + jl) * 512:(k * 4 + jl + 1) * 512],
                                   start=True, stop=True))

        emit_scores(0)
        pvidx = 0
        pvb = None
        for idx, (jl, k, i, jg) in enumerate(steps):
            epar = idx % 2
            sb_ = psum[epar]
            if i == 0:
                pvb = [psum[4 + 2 * (pvidx % 2)], psum[5 + 2 * (pvidx % 2)]]
                pvidx += 1
            we = rg_ebuf[epar].take(("e", sch.cnt["A"]))
            av = sch.A([("T", sb_.writer)] + we, lambda e, p=epar, b=sb_:
                       nc.scalar.activation(ebuf[p][:, :], b.ap[:, 0:512],
                                            AF.Exp, bias=cvals[:, 3:4],
                                            scale=cvals[:, 2:3]))
            sb_.readers.append(("A", av))
            eready = ("A", av)
            if i == jg:
                vv = sch.V([eready, tri_ready], lambda e, p=epar:
                           nc.vector.tensor_tensor(ebuf[p][:, :], ebuf[p][:, :],
                                                   tri[:, :], OP.mult))
                eready = ("V", vv)
            if idx + 1 < len(steps):
                emit_scores(idx + 1)
            for g4 in range(4):
                w2 = [eready, ("V", vaug_ready[k][i])]
                bank = pvb[g4 // 2]
                if i == 0 and g4 == 0:
                    w2 += pvb[0].readers + pvb[1].readers
                    pvb[0].readers = []
                    pvb[1].readers = []
                tv2 = sch.T(w2, lambda e, b=bank, p=epar, g4=g4, k=k, i=i, jg=jg:
                            nc.tensor.matmul(
                                b.ap[:, (g4 % 2) * 129:(g4 % 2) * 129 + 129],
                                ebuf[p][:, g4 * 128:(g4 + 1) * 128],
                                vaug[:, (i * c.NHKV + k) * 129:
                                     (i * c.NHKV + k) * 129 + 129],
                                start=(i == 0 and g4 % 2 == 0),
                                stop=(i == jg and g4 % 2 == 1)))
                rg_ebuf[epar].use("T", tv2)
                if i == jg:
                    bank.writer = tv2
            if i == jg:
                rv = 0
                for bb in range(2):
                    dn = pvb[bb].ap[:, 0:258].rearrange("p (g v) -> p g v", v=129)[
                        :, :, 128:129]
                    rv = sch.V([("T", pvb[0].writer), ("T", pvb[1].writer)],
                               lambda e, d=dn, bb=bb: nc.vector.reciprocal(
                                   arcp[:, bb * 2:bb * 2 + 2], d))
                    pvb[bb].readers.append(("V", rv))
                wa = rg_hs[jl // 2].take(("ao", g)) if k == 0 else []
                for g4 in range(4):
                    bank = pvb[g4 // 2]
                    hcol = (4 * k + g4) * 128
                    vv = sch.V([("T", bank.writer), ("V", rv)] + (wa if g4 == 0 else []),
                               lambda e, b=bank, g4=g4, jl=jl, hc=hcol:
                               nc.vector.tensor_scalar(
                                   attn_out[jl][:, hc:hc + 128],
                                   b.ap[:, (g4 % 2) * 129:(g4 % 2) * 129 + 128],
                                   arcp[:, g4:g4 + 1], 0.0, OP.mult, OP.add))
                    bank.readers.append(("V", vv))
                    rg_hs[jl // 2].use("V", vv)
                attn_ready[jl][k] = sch.cnt["V"]
        rg_qtar.use("T", sch.cnt["T"])

        aoT_blk_ready = [0] * nblk
        wq0 = rg_qtar.take(("aoT", g))
        for jl in range(c.GT):
            for cq, vv in transpose_tile(
                    attn_out[jl], aoT, jl,
                    lambda ci, jl=jl: [("V", attn_ready[jl][ci // 4])],
                    wq0 if jl == 0 else []):
                if jl == c.GT - 1 and (cq % 2 == 1 or cq == c.NCH // 4 - 1):
                    aoT_blk_ready[min(cq // 2, nblk - 1)] = vv
            rg_hs[jl // 2].use("T", sch.cnt["T"])
        for b in range(nblk):
            if aoT_blk_ready[b] == 0:
                aoT_blk_ready[b] = aoT_blk_ready[-1]
        xdma = [0] * c.GT
        hw0 = [[] for _ in range(c.GT)]
        for jl in range(c.GT):
            row0 = (g * c.GT + jl) * 128
            wts = rg_hs[jl].take(("h", g))
            if jl < 2:
                xdma[jl] = sch.X(wts, lambda e, r=row0, jl=jl:
                                 e.dma_start(h[jl][:, :], x_d[r:r + 128, :]))
            else:
                xdma[jl] = axv[jl]
                hw0[jl] = wts
        h_add = [[0] * (c.D // 512) for _ in range(c.GT)]
        for s in range(c.D // 512):
            base = (s % 2) * 4
            for b in range(nblk):
                nch = min(8, c.NCH - b * 8)
                buf, wv = stream_block(wo_d, b * 8 * 128, nch, s * 512, 512)
                for cc in range(nch):
                    ci = b * 8 + cc
                    for jl in range(c.GT):
                        bank = psum[base + jl]
                        mm(bank, bank.ap[:, 0:512],
                           aoT[:, ci * 512 + jl * 128:ci * 512 + (jl + 1) * 128],
                           wblk[buf][:, cc * 512:(cc + 1) * 512],
                           start=(ci == 0), stop=(ci == c.NCH - 1),
                           waits=[wv, ("V", aoT_blk_ready[b])])
                rg_wblk[buf].use("T", sch.cnt["T"])
            for jl in range(c.GT):
                bank = psum[base + jl]
                vv = sch.V([("T", bank.writer), xdma[jl]]
                           + (hw0[jl] if s == 0 else []),
                           lambda e, b=bank, jl=jl, s=s: nc.vector.tensor_tensor(
                               h[jl][:, s * 512:(s + 1) * 512], b.ap[:, 0:512],
                               h[jl][:, s * 512:(s + 1) * 512], OP.add))
                bank.readers.append(("V", vv))
                h_add[jl][s] = vv
                rg_hs[jl].use("V", vv)
        rg_qtar.use("T", sch.cnt["T"])

        hnT_blk_ready = [0] * nblk
        wq0 = rg_qtar.take(("hnT", g))
        for jl in range(c.GT):
            wxn = rg_xnhn.take(("E", g, jl))
            v1, a3 = layer_norm(h[jl], jl, [("V", h_add[jl][-1])], xnhn, wxn)
            rg_hs[jl].use("V", v1)
            rg_hs[jl].use("A", a3)
            for cq, vv in transpose_tile(xnhn, hnT, jl, lambda ci, a3=a3: [("A", a3)],
                                         wq0 if jl == 0 else []):
                if jl == c.GT - 1 and (cq % 2 == 1 or cq == c.NCH // 4 - 1):
                    hnT_blk_ready[min(cq // 2, nblk - 1)] = vv
            rg_xnhn.use("T", sch.cnt["T"])
        for b in range(nblk):
            if hnT_blk_ready[b] == 0:
                hnT_blk_ready[b] = hnT_blk_ready[-1]
        bank_m = psum[2]
        for ci in range(c.NCH):
            mm(bank_m, bank_m.ap[0:c.EW, 0:512],
               awr[:, ci * c.EW:(ci + 1) * c.EW], hnT[:, ci * 512:(ci + 1) * 512],
               start=(ci == 0), stop=(ci == c.NCH - 1),
               waits=[awr_ready, ("V", hnT_blk_ready[ci // 8])])
        rg_qtar.use("T", sch.cnt["T"])
        wml = rg_midlog.take(("ml", g))
        aml = sch.A([("T", bank_m.writer)] + wml, lambda e: nc.scalar.activation(
            midlog_bf[:, :], bank_m.ap[0:c.EW, 0:512], AF.Identity))
        bank_m.readers.append(("A", aml))
        bank_t = psum[3]
        for jl in range(c.GT):
            w = [("A", aml), iden_ready] + (bank_t.readers if jl == 0 else [])
            if jl == 0:
                bank_t.readers = []
            tv = sch.T(w, lambda e, jl=jl: nc.tensor.matmul(
                bank_t.ap[:, jl * c.EW:(jl + 1) * c.EW],
                midlog_bf[:, jl * 128:(jl + 1) * 128],
                iden[0:c.EW, 0:c.EW], start=True, stop=True))
            rg_midlog.use("T", tv)
            bank_t.writer = tv
        vml = sch.V([("T", bank_t.writer)], lambda e: nc.vector.tensor_copy(
            midlog_sb[:, 0:c.GT * c.EW], bank_t.ap[:, 0:c.GT * c.EW]))
        bank_t.readers.append(("V", vml))

        bank_gt = psum[4]
        for jl in range(c.GT):
            lg = midlog_sb[:, jl * c.EW + c.NE * c.R:(jl + 1) * c.EW]
            md = midlog_sb[:, jl * c.EW:jl * c.EW + c.NE * c.R]
            m1, m2 = gsc[:, 0:1], gsc[:, 1:2]
            eq, l2 = gsc[:, 4:4 + c.NE], gsc[:, 8:8 + c.NE]
            mask, em = gsc[:, 12:12 + c.NE], gsc[:, 16:16 + c.NE]
            g4v = gsc[:, 20:20 + c.NE]
            p = jl % 2
            c1 = sch.V([("V", vml)], lambda e, lg=lg, m1=m1:
                       nc.vector.reduce_max(m1, lg, AX, negate=True))
            c2 = sch.V([("V", c1)], lambda e, lg=lg, m1=m1, eq=eq:
                       nc.vector.tensor_scalar(
                           eq, lg, m1[:, 0:1], 0.0, OP.add, OP.is_equal))
            c3 = sch.V([("V", c2)], lambda e, lg=lg, eq=eq, l2=l2:
                       nc.vector.scalar_tensor_tensor(
                           l2, eq, -1e30, lg, OP.mult, OP.add))
            c4 = sch.V([("V", c3)], lambda e, l2=l2, m2=m2:
                       nc.vector.reduce_max(m2, l2, AX, negate=True))
            vmask = sch.V([("V", c4)], lambda e, lg=lg, m2=m2, mask=mask:
                          nc.vector.tensor_scalar(
                              mask, lg, m2[:, 0:1], 0.0, OP.add, OP.is_ge))
            aex = sch.A([("V", vmask)], lambda e, lg=lg, m1=m1, p=p:
                        nc.scalar.activation(gex[p][:, :], lg, AF.Exp, bias=m1[:, 0:1]))
            c5 = sch.V([("A", aex), ("V", vmask)], lambda e, em=em, p=p, mask=mask:
                       nc.vector.tensor_tensor(em, gex[p][:, :], mask, OP.mult))
            c6 = sch.V([("V", c5)], lambda e, em=em:
                       nc.vector.reduce_sum(gden[:, 0:1], em, AX))
            c7 = sch.V([("V", c6)], lambda e:
                       nc.vector.reciprocal(gden[:, 1:2], gden[:, 0:1]))
            vg = sch.V([("V", c7)], lambda e, em=em, g4v=g4v: nc.vector.tensor_scalar(
                g4v, em, gden[:, 1:2], 4.0, OP.mult, OP.mult))
            wgm = rg_gmid.take(("gm", g, jl))
            vgm = 0
            for e4 in range(c.NE):
                vgm = sch.V([("V", vg)] + (wgm if e4 == 0 else []),
                            lambda e, md=md, e4=e4, g4v=g4v: nc.vector.tensor_scalar(
                                gmid[:, e4 * c.R:(e4 + 1) * c.R],
                                md[:, e4 * c.R:(e4 + 1) * c.R],
                                g4v[:, e4:e4 + 1], 0.0, OP.mult, OP.add))
            w = [("V", vgm), iden_ready] + (bank_gt.readers if jl == 0 else [])
            if jl == 0:
                bank_gt.readers = []
            sch.T(w, lambda e, jl=jl: nc.tensor.matmul(
                bank_gt.ap[0:c.NE * c.R, jl * 128:(jl + 1) * 128],
                gmid[:, :], iden[:, :], start=True, stop=True))
            tv = sch.T([], lambda e, jl=jl: nc.tensor.matmul(
                bank_gt.ap[c.NE * c.R:2 * c.NE * c.R, jl * 128:(jl + 1) * 128],
                gmid[:, :], iden[:, :], start=True, stop=True))
            rg_gmid.use("T", tv)
            bank_gt.writer = tv
        wgt = rg_gmidT.take(("gt", g))
        vgt = sch.V([("T", bank_gt.writer)] + wgt, lambda e: nc.vector.tensor_copy(
            gmidT[:, 0:c.GT * 128], bank_gt.ap[:, 0:c.GT * 128]))
        bank_gt.readers.append(("V", vgt))

        for jl in range(c.GT):
            last_add = 0
            for sl in range(c.D // 512):
                bank = psum[5 + sl % 2]
                w = [("V", vgt), bcat_ready] + bank.readers
                bank.readers = []
                if sl < c.D // 1024:
                    rhs = bcat[0:64, sl * 512:(sl + 1) * 512]
                    lhs = gmidT[0:64, jl * 128:(jl + 1) * 128]
                else:
                    rhs = bcat[64:128, (sl - c.D // 1024) * 512:
                               (sl - c.D // 1024 + 1) * 512]
                    lhs = gmidT[64:128, jl * 128:(jl + 1) * 128]
                tv = sch.T(w, lambda e, b=bank, l=lhs, r=rhs: nc.tensor.matmul(
                    b.ap[:, 0:512], l, r, start=True, stop=True))
                bank.writer = tv
                rg_gmidT.use("T", tv)
                vv = sch.V([("T", tv), ("V", h_add[jl][sl])],
                           lambda e, b=bank, jl=jl, sl=sl:
                           nc.vector.tensor_tensor(
                               h[jl][:, sl * 512:(sl + 1) * 512], b.ap[:, 0:512],
                               h[jl][:, sl * 512:(sl + 1) * 512], OP.add))
                bank.readers.append(("V", vv))
                last_add = vv
                rg_hs[jl].use("V", vv)
            row0 = (g * c.GT + jl) * 128
            xv = sch.X([("V", last_add)], lambda e, r=row0, jl=jl:
                       e.dma_start(out_d[r:r + 128, :], h[jl][:, :]))
            rg_hs[jl].use(*xv)

    sch.raw("sync", [(k, sch.cnt[k]) for k in DMA_FAM["X"]])
    sch.raw("gpsimd", [(k, sch.cnt[k]) for k in DMA_FAM["W"]])
    sch.raw("scalar", [(k, sch.cnt[k]) for k in DMA_FAM["Z"]])

    with nc.Block() as block:
        @block.tensor
        def _(e):
            for f in sch.q["tensor"]:
                f(e)

        @block.scalar
        def _(e):
            for f in sch.q["scalar"]:
                f(e)

        @block.vector
        def _(e):
            for f in sch.q["vector"]:
                f(e)

        @block.sync
        def _(e):
            for f in sch.q["sync"]:
                f(e)

        @block.gpsimd
        def _(e):
            for f in sch.q["gpsimd"]:
                f(e)

    return nc


def make_tables(cfg):
    c = cfg
    ex = np.arange(0, 128, 2, dtype=np.float32) / np.float32(128.0)
    inv = (np.float32(1.0) / (np.float32(10000.0) ** ex)).astype(np.float32)
    t = np.arange(c.S, dtype=np.float32)
    fr = t[:, None] * inv[None, :]
    cosv = np.cos(fr).astype(np.float32)
    sinv = np.sin(fr).astype(np.float32)
    cost = np.concatenate([cosv.T, cosv.T], axis=0)
    sint = np.concatenate([-sinv.T, sinv.T], axis=0)
    iden = np.eye(128, dtype=np.float32).astype(BFNP)
    m = np.arange(128)[:, None]
    q = np.arange(128)[None, :]
    trib = (q >= m).astype(np.float32)
    tri = np.ascontiguousarray(np.concatenate([trib] * 4, axis=1)).astype(BFNP)
    return cost, sint, iden, tri


def make_inputs(cfg, x, wq, wk, wv, wo, A, B, Wr):
    c = cfg
    cost, sint, iden, tri = make_tables(c)
    awr = np.ascontiguousarray(np.concatenate(
        [np.ascontiguousarray(A.transpose(2, 0, 1)).reshape(c.D, c.NE * c.R),
         np.ascontiguousarray(Wr.T)], axis=1), dtype=np.float32)
    bc = np.ascontiguousarray(B.transpose(0, 2, 1)).reshape(c.NE * c.R, c.D)
    bcat = np.ascontiguousarray(
        np.concatenate([bc[:, :c.D // 2], bc[:, c.D // 2:]], axis=0), dtype=np.float32)
    return dict(x=np.ascontiguousarray(x, dtype=np.float32),
                wq=np.ascontiguousarray(wq, dtype=np.float32),
                wk=np.ascontiguousarray(wk, dtype=np.float32),
                wv=np.ascontiguousarray(wv, dtype=np.float32),
                wo=np.ascontiguousarray(wo, dtype=np.float32),
                awr=awr, bcat=bcat, cost=cost, sint=sint, iden=iden, tri=tri)


def kernel(L0_attn_x, L0_attn_Wq, L0_attn_Wk, L0_attn_Wv, L0_attn_Wo,
           L0_ffn_A, L0_ffn_B, L0_ffn_Wr):
    cfg = CFG(D=4096, S=2048, NHQ=32, NHKV=8)
    x = np.asarray(L0_attn_x, dtype=np.float32)
    nb = x.shape[0]
    base = make_inputs(cfg, x[0],
                       np.asarray(L0_attn_Wq), np.asarray(L0_attn_Wk),
                       np.asarray(L0_attn_Wv), np.asarray(L0_attn_Wo),
                       np.asarray(L0_ffn_A), np.asarray(L0_ffn_B),
                       np.asarray(L0_ffn_Wr))
    in_maps = []
    for i in range(nb):
        m = dict(base)
        m["x"] = np.ascontiguousarray(x[i])
        in_maps.append(m)
    nc = build_program(cfg)
    res = run_bass_kernel_spmd(nc, in_maps, core_ids=list(range(nb)))
    out = np.stack([np.asarray(res.results[i]["out"]) for i in range(nb)], axis=0)
    return out.astype(np.float32)


# version 03
# speedup vs baseline: 1.3442x; 1.0685x over previous
import numpy as np
import ml_dtypes
import concourse.bass as bass
import concourse.mybir as mybir
from concourse.bass_utils import run_bass_kernel_spmd

BFNP = ml_dtypes.bfloat16
F32 = mybir.dt.float32
BF16 = mybir.dt.bfloat16
AX = mybir.AxisListType.X
OP = mybir.AluOpType
AF = mybir.ActivationFunctionType

NEG_SHIFT = -16.0
EPS = 1e-5


class CFG:
    def __init__(self, D, S, NHQ, NHKV):
        self.D = D
        self.S = S
        self.NHQ = NHQ
        self.NHKV = NHKV
        self.GROUP_T = 512
        self.HD = 128
        self.NCH = D // 128
        self.GT = 4
        self.NG = S // 512
        self.NT = S // 128
        self.DKV = NHKV * 128
        self.NE = 4
        self.R = 16
        self.EW = self.NE * self.R + self.NE
        self.SCL = float(1.0 / np.sqrt(128.0))
        assert self.NHQ // self.NHKV == 4 and self.NHQ % 4 == 0
        assert self.D % 512 == 0 and self.S % 512 == 0


ENGS = ("tensor", "scalar", "vector", "sync", "gpsimd")


SEM_NAMES = ("T", "A", "V", "X0", "X1", "X2", "X3", "Z0", "Z1", "W0", "W1")
DMA_FAM = {"X": ("X0", "X1", "X2", "X3"), "Z": ("Z0", "Z1"), "W": ("W0", "W1")}


class Sched:
    def __init__(self):
        self.q = {k: [] for k in ENGS}
        self.cnt = {k: 0 for k in SEM_NAMES}
        self.rr = {k: 0 for k in DMA_FAM}
        self.waited = {k: {} for k in ENGS}
        self.semh = {}

    def op(self, eng, semkey, inc, waits, fn):
        self.cnt[semkey] += inc
        val = self.cnt[semkey]
        wl = {}
        for s, v in waits:
            if v and v > 0:
                wl[s] = max(wl.get(s, 0), v)
        wl = sorted(wl.items())

        def run(e, wl=wl, fn=fn, semkey=semkey, inc=inc, eng=eng):
            wd = self.waited[eng]
            for s, v in wl:
                if wd.get(s, 0) < v:
                    e.wait_ge(self.semh[s], v)
                    wd[s] = v
            ins = fn(e)
            if inc:
                ins.then_inc(self.semh[semkey], inc)

        self.q[eng].append(run)
        return val

    def T(self, waits, fn):
        return self.op("tensor", "T", 1, waits, fn)

    def A(self, waits, fn):
        return self.op("scalar", "A", 1, waits, fn)

    def V(self, waits, fn):
        return self.op("vector", "V", 1, waits, fn)

    def dma(self, eng, fam, waits, fn):
        k = DMA_FAM[fam][self.rr[fam] % len(DMA_FAM[fam])]
        self.rr[fam] += 1
        w = list(waits) + [(k, self.cnt[k])]
        v = self.op(eng, k, 16, w, fn)
        return (k, v)

    def X(self, waits, fn):
        return self.dma("sync", "X", waits, fn)

    def Z(self, waits, fn):
        return self.dma("scalar", "Z", waits, fn)

    def W(self, waits, fn):
        return self.dma("gpsimd", "W", waits, fn)

    def raw(self, eng, waits):
        wl = {}
        for s, v in waits:
            if v and v > 0:
                wl[s] = max(wl.get(s, 0), v)
        wl = sorted(wl.items())

        def run(e, wl=wl, eng=eng):
            wd = self.waited[eng]
            for s, v in wl:
                if wd.get(s, 0) < v:
                    e.wait_ge(self.semh[s], v)
                    wd[s] = v

        self.q[eng].append(run)


class Region:

    def __init__(self):
        self.users = []
        self.phase = None

    def use(self, sem, val):
        self.users.append((sem, val))

    def take(self, phase):
        if self.phase == phase:
            return []
        self.phase = phase
        u = self.users
        self.users = []
        return u


class Bank:
    def __init__(self, ap):
        self.ap = ap
        self.writer = 0
        self.readers = []


def build_program(cfg):
    c = cfg
    nc = bass.Bass("TRN2", target_bir_lowering=False)

    x_d = nc.dram_tensor("x", [c.S, c.D], F32, kind="ExternalInput").ap()
    wq_d = nc.dram_tensor("wq", [c.D, c.D], F32, kind="ExternalInput").ap()
    wk_d = nc.dram_tensor("wk", [c.D, c.DKV], F32, kind="ExternalInput").ap()
    wv_d = nc.dram_tensor("wv", [c.D, c.DKV], F32, kind="ExternalInput").ap()
    wo_d = nc.dram_tensor("wo", [c.D, c.D], F32, kind="ExternalInput").ap()
    awr_d = nc.dram_tensor("awr", [c.D, c.EW], F32, kind="ExternalInput").ap()
    bcat_d = nc.dram_tensor("bcat", [128, c.D // 2], F32, kind="ExternalInput").ap()
    cost_d = nc.dram_tensor("cost", [128, c.S], F32, kind="ExternalInput").ap()
    sint_d = nc.dram_tensor("sint", [128, c.S], F32, kind="ExternalInput").ap()
    iden_d = nc.dram_tensor("iden", [128, 128], BF16, kind="ExternalInput").ap()
    tri_d = nc.dram_tensor("tri", [128, 512], BF16, kind="ExternalInput").ap()
    out_d = nc.dram_tensor("out", [c.S, c.D], F32, kind="ExternalOutput").ap()
    nblk_ = (c.NCH + 7) // 8
    NBLK = nblk_ * (c.NHQ // 4 + (c.NHKV + 3) // 4 + (c.DKV + 511) // 512 + c.D // 512)
    wscr_d = nc.dram_tensor("wscr", [NBLK, 128, 8 * 512], BF16, kind="Internal").ap()

    sch = Sched()

    cursor = [0]
    backing = nc.sbuf_tensor("backing", [128, 212800], mybir.dt.uint8)
    backing_h = backing.__enter__()
    base_addr = nc.lookup_mloc(backing_h).addr

    def at(name, shape, dtype, off=None):
        per_part = int(np.prod(shape[1:])) * (4 if dtype == F32 else 2)
        if off is None:
            off = cursor[0]
            cursor[0] = off + ((per_part + 31) // 32) * 32
        assert off + per_part <= 212800
        return nc.alloc_sbuf_tensor_at(name, shape, dtype, offset=base_addr + off)

    def place(nbytes):
        off = cursor[0]
        cursor[0] += ((nbytes + 31) // 32) * 32
        return off

    TLB = c.D * 4
    kt = at("kt", [128, c.NHKV * c.S], BF16)
    vaug = at("vaug", [128, c.NT * c.NHKV * 129], BF16)
    arena1_off = place(2 * TLB)
    qtar_off = place(c.NHKV * 4 * c.GROUP_T * 2)
    hextra_off = place(2 * TLB)

    h = [at("h0", [128, c.D], F32, arena1_off),
         at("h1", [128, c.D], F32, arena1_off + TLB),
         at("h2", [128, c.D], F32, hextra_off),
         at("h3", [128, c.D], F32, hextra_off + TLB)]
    attn_out = [at(f"ao{j}", [128, c.D], BF16, arena1_off + j * c.D * 2)
                for j in range(c.GT)]
    xnT = at("xnT", [128, c.NCH * c.GROUP_T], BF16, arena1_off)
    qt = at("qt", [128, c.NHKV * 4 * c.GROUP_T], BF16, qtar_off)
    aoT = at("aoT", [128, c.NCH * c.GROUP_T], BF16, qtar_off)
    hnT = at("hnT", [128, c.NCH * c.GROUP_T], BF16, qtar_off)

    wblk = [at(f"wblk{i}", [128, 8 * 512], BF16) for i in range(2)]
    xnhn = at("xnhn", [128, c.D], BF16)
    qswp = [at(f"qswp{i}", [128, 512], BF16) for i in range(2)]
    t1b = [at(f"t1b{i}", [128, 512], BF16) for i in range(2)]
    ebuf = [at(f"ebuf{i}", [128, 512], BF16) for i in range(2)]
    cosg = at("cosg", [128, 512], F32)
    sing = at("sing", [128, 512], F32)
    awr = at("awr", [128, c.NCH * c.EW], BF16)
    bcat = at("bcat", [128, c.D // 2], BF16)
    iden = at("iden", [128, 128], BF16)
    tri = at("tri", [128, 512], BF16)
    midlog_bf = at("midlogbf", [c.EW, c.GROUP_T], BF16)
    midlog_sb = at("midlogsb", [128, c.GT * c.EW], F32)
    gmid = at("gmid", [128, c.NE * c.R], BF16)
    gmidT = at("gmidT", [128, c.GT * 128], BF16)
    bst = at("bst", [128, 6 * (c.D // 512)], F32)
    agg = [at(f"agg{i}", [128, 2], F32) for i in range(2)]
    lnv = [at(f"lnv{i}", [128, 1], F32) for i in range(2)]
    rstd = [at(f"rstd{i}", [128, 1], F32) for i in range(2)]
    negmr = [at(f"negmr{i}", [128, 1], F32) for i in range(2)]
    gsc = at("gsc", [128, 24], F32)
    gex = [at(f"gex{i}", [128, c.NE], F32) for i in range(2)]
    gden = at("gden", [128, 2], F32)
    arcp = at("arcp", [128, 4], F32)
    cvals = at("cvals", [128, 4], F32)

    psum = []
    for i in range(8):
        cm = nc.psum_tensor(f"pb{i}", [128, 512], F32)
        psum.append(Bank(cm.__enter__()))

    for s in SEM_NAMES:
        cm = nc.semaphore(f"sem{s}")
        sch.semh[s] = cm.__enter__()

    rg_qtar = Region()
    rg_hs = [Region() for _ in range(4)]
    rg_xnhn = Region()
    rg_wblk = [Region() for _ in range(2)]
    rg_qswp = [Region() for _ in range(2)]
    rg_ebuf = [Region() for _ in range(2)]
    rg_tab = Region()
    rg_stats = [Region() for _ in range(2)]
    rg_midlog = Region()
    rg_gmid = Region()
    rg_gmidT = Region()

    blkid = [0]
    gblk = [0]
    cur_g = [0]
    scratch_ready = [None] * NBLK

    def stream_block(dram, r0, nch, c0, cw):
        buf = blkid[0] % 2
        blkid[0] += 1
        bidx = gblk[0]
        gblk[0] += 1
        waits = rg_wblk[buf].take(("blk", blkid[0]))
        ncols = nch * cw
        if cur_g[0] == 0:
            src = dram[r0:r0 + nch * 128, c0:c0 + cw].rearrange("(a p) v -> p a v", p=128)
            dst = wblk[buf][:, 0:ncols].rearrange("p (a v) -> p a v", v=cw)
            wv = sch.W(waits, lambda e, dst=dst, src=src: e.dma_start(dst, src))
            if c.NG > 1:
                sv = sch.X([wv], lambda e, bidx=bidx, buf=buf, ncols=ncols:
                           e.dma_start(wscr_d[bidx:bidx + 1, :, 0:ncols],
                                       wblk[buf][:, 0:ncols]))
                scratch_ready[bidx] = sv
                rg_wblk[buf].use(*sv)
        else:
            wv = sch.X(waits + [scratch_ready[bidx]],
                       lambda e, bidx=bidx, buf=buf, ncols=ncols:
                       e.dma_start(wblk[buf][:, 0:ncols],
                                   wscr_d[bidx:bidx + 1, :, 0:ncols]))
        return buf, wv

    def mm(bank, out_ap, lhsT, rhs, start, stop, waits):
        w = list(waits)
        if start:
            w += bank.readers
            bank.readers = []
        tv = sch.T(w, lambda e, o=out_ap, l=lhsT, r=rhs, st=start, sp=stop:
                   nc.tensor.matmul(o, l, r, start=st, stop=sp))
        if stop:
            bank.writer = tv
        return tv

    def evac_quad(bank, dstT, cq, jl, extra_waits):
        src = bank.ap[:, 0:512].rearrange("p (a q) -> p a q", q=128)
        dst = dstT[:, cq * 4 * c.GROUP_T:(cq * 4 + 4) * c.GROUP_T] \
            .rearrange("p (a t) -> p a t", t=c.GROUP_T)[:, :, jl * 128:(jl + 1) * 128]
        vv = sch.V([("T", bank.writer)] + extra_waits,
                   lambda e, d=dst, s=src: nc.vector.tensor_copy(d, s))
        bank.readers.append(("V", vv))
        return vv

    def transpose_tile(src_bf, dstT, jl, ready_fn, first_evac_waits):
        res = []
        for cq in range(c.NCH // 4):
            bank = psum[cq % 2]
            tv = 0
            for ciq in range(4):
                ci = cq * 4 + ciq
                w = ready_fn(ci)
                if ciq == 0:
                    w = w + bank.readers
                    bank.readers = []
                tv = sch.T(w, lambda e, b=bank, ciq=ciq, ci=ci, s=src_bf:
                           nc.tensor.matmul(b.ap[:, ciq * 128:(ciq + 1) * 128],
                                            s[:, ci * 128:(ci + 1) * 128], iden[:, :],
                                            start=True, stop=True))
            bank.writer = tv
            ew = first_evac_waits if cq == 0 else []
            res.append((cq, evac_quad(bank, dstT, cq, jl, ew)))
        return res

    def layer_norm(xt, jl, extra_waits, dst_bf, dst_waits):
        p = jl % 2
        wst = rg_stats[p].take(("ln", sch.cnt["V"]))
        v1 = 0
        for su in range(c.D // 512):
            v1 = sch.V(extra_waits + wst if su == 0 else [],
                       lambda e, s=xt, su=su: nc.vector.bn_stats(
                           bst[:, 6 * su:6 * (su + 1)], s[:, 512 * su:512 * (su + 1)]))
        v2 = sch.V([("V", v1)], lambda e, p=p: nc.vector.bn_aggr(agg[p][:, :], bst[:, :]))
        a1 = sch.A([("V", v2)], lambda e, p=p: nc.scalar.activation(
            lnv[p][:, :], agg[p][:, 1:2], AF.Ln, bias=cvals[:, 0:1]))
        a2 = sch.A([("A", a1)], lambda e, p=p: nc.scalar.activation(
            rstd[p][:, :], lnv[p][:, :], AF.Exp, scale=cvals[:, 1:2]))
        v3 = sch.V([("A", a2), ("V", v2)], lambda e, p=p: nc.vector.tensor_scalar(
            negmr[p][:, :], agg[p][:, 0:1], rstd[p][:, 0:1], -1.0, OP.mult, OP.mult))
        rg_stats[p].use("V", v3)
        a3 = sch.A([("V", v3), ("A", a2)] + dst_waits, lambda e, s=xt, d=dst_bf, p=p:
                   nc.scalar.activation(d[:, :], s[:, :], AF.Identity,
                                        bias=negmr[p][:, 0:1], scale=rstd[p][:, 0:1]))
        rg_stats[p].use("A", a3)
        return v1, a3

    def rope(dst_ap, swap_hi, swap_lo, cos_ap, sin_ap, sw_ap, t1_ap, sb, waits, prod):
        wq0 = rg_qswp[sb].take(("rp", sch.rr["Z"]))
        z1 = sch.Z([("A", prod)] + wq0 + waits,
                   lambda e, s=swap_hi, sb=sb: e.dma_start(qswp[sb][0:64, :], s))
        z2 = sch.Z([("A", prod)] + wq0,
                   lambda e, s=swap_lo, sb=sb: e.dma_start(qswp[sb][64:128, :], s))
        v1 = sch.V([z1, z2, ("A", prod)],
                   lambda e, t=t1_ap, sw=sw_ap, s=sin_ap:
                   nc.vector.tensor_tensor(t, sw, s, OP.mult))
        v2 = sch.V([("V", v1)], lambda e, sw=sw_ap, d=dst_ap, co=cos_ap:
                   nc.vector.tensor_tensor(sw, d, co, OP.mult))
        v3 = sch.V([("V", v2)], lambda e, d=dst_ap, t=t1_ap, sw=sw_ap:
                   nc.vector.tensor_tensor(d, t, sw, OP.add))
        rg_qswp[sb].use("V", v3)
        return v3

    def qt_head(k, g4, p0=0, p1=128):
        return qt[p0:p1, :].rearrange("p (k j q) -> p k j q", k=c.NHKV, q=512)[
            :, k:k + 1, :, g4 * 128:(g4 + 1) * 128]

    def four_dim(ap2d):
        return ap2d.rearrange("p (k j q) -> p k j q", k=1, q=128)

    xc1 = sch.X([], lambda e: e.dma_start(iden[:, :], iden_d[:, :]))
    xc2 = sch.X([], lambda e: e.dma_start(tri[:, :], tri_d[:, :]))
    asrc = awr_d[:, :].rearrange("(a p) v -> p a v", p=128)
    adst = awr[:, :].rearrange("p (a v) -> p a v", v=c.EW)
    wc1 = sch.W([], lambda e, d=adst, s=asrc: e.dma_start(d, s))
    wc2 = sch.W([], lambda e: e.dma_start(bcat[:, :], bcat_d[:, :]))
    ones_ap = vaug[:, :].rearrange("p (s v) -> p s v", v=129)[:, :, 128:129]
    vo = sch.V([], lambda e, a=ones_ap: nc.vector.memset(a, 1.0))
    sch.V([], lambda e: nc.vector.memset(cvals[:, 0:1], EPS))
    sch.V([], lambda e: nc.vector.memset(cvals[:, 1:2], -0.5))
    sch.V([], lambda e: nc.vector.memset(cvals[:, 2:3], c.SCL))
    cvr = sch.V([], lambda e: nc.vector.memset(cvals[:, 3:4], NEG_SHIFT))
    sch.raw("scalar", [("V", cvr)])
    tb = sch.X([], lambda e: e.dma_start(cosg[:, :], cost_d[:, 0:512]))
    tb = sch.X([], lambda e: e.dma_start(sing[:, :], sint_d[:, 0:512]))
    rg_tab.use(*tb)
    iden_ready = xc1
    tri_ready = xc2
    awr_ready = wc1
    bcat_ready = wc2

    kt_ready = [[0] * c.NG for _ in range(c.NHKV)]
    vaug_ready = [[0] * c.NT for _ in range(c.NHKV)]
    qsw_ready = [0] * c.NHKV

    for g in range(c.NG):
        cur_g[0] = g
        gblk[0] = 0
        nblk = (c.NCH + 7) // 8
        xnT_blk_ready = [0] * nblk
        axv = [0] * c.GT
        for jl in range(c.GT):
            row0 = (g * c.GT + jl) * 128
            slot = 2 + (jl % 2)
            wts = rg_hs[slot].take(("Ax", g, jl))
            xv = sch.X(wts, lambda e, r=row0, s=slot: e.dma_start(h[s][:, :],
                                                                  x_d[r:r + 128, :]))
            axv[jl] = xv
            rg_hs[slot].use(*xv)
            wxn = rg_xnhn.take(("A", g, jl))
            v1, a3 = layer_norm(h[slot], jl, [xv], xnhn, wxn)
            rg_hs[slot].use("V", v1)
            rg_hs[slot].use("A", a3)
            warena = (rg_hs[0].take(("xnT", g)) + rg_hs[1].take(("xnT", g))) if jl == 0 else []
            for cq, vv in transpose_tile(xnhn, xnT, jl, lambda ci, a3=a3: [("A", a3)],
                                         warena):
                if jl == c.GT - 1 and (cq % 2 == 1 or cq == c.NCH // 4 - 1):
                    xnT_blk_ready[min(cq // 2, nblk - 1)] = vv
            rg_xnhn.use("T", sch.cnt["T"])
        for b in range(nblk):
            if xnT_blk_ready[b] == 0:
                xnT_blk_ready[b] = xnT_blk_ready[-1]

        wtab = [tb]
        for s in range(c.NHQ // 4):
            base = (s % 2) * 4
            for b in range(nblk):
                nch = min(8, c.NCH - b * 8)
                buf, wv = stream_block(wq_d, b * 8 * 128, nch, s * 512, 512)
                for cc in range(nch):
                    ci = b * 8 + cc
                    for hh in range(4):
                        bank = psum[base + hh]
                        mm(bank, bank.ap[:, 0:512],
                           wblk[buf][:, cc * 512 + hh * 128:cc * 512 + (hh + 1) * 128],
                           xnT[:, ci * 512:(ci + 1) * 512],
                           start=(ci == 0), stop=(ci == c.NCH - 1),
                           waits=[wv, ("V", xnT_blk_ready[b])])
                rg_wblk[buf].use("T", sch.cnt["T"])
            for hh in range(4):
                bank = psum[base + hh]
                dst = qt_head(s, hh)
                wq0 = rg_qtar.take(("qt", g)) if (s == 0 and hh == 0) else []
                a1 = sch.A([("T", bank.writer)] + wq0,
                           lambda e, d=dst, b=bank: nc.scalar.activation(
                               d, b.ap[:, 0:512], AF.Identity))
                bank.readers.append(("A", a1))
                sb = hh % 2
                rope(dst, qt_head(s, hh, 64, 128), qt_head(s, hh, 0, 64),
                     four_dim(cosg[:, :]), four_dim(sing[:, :]),
                     four_dim(qswp[sb][:, :]), four_dim(t1b[sb][:, :]),
                     sb, wtab, a1)
            qsw_ready[s] = sch.cnt["V"]

        for s in range((c.NHKV + 3) // 4):
            kh = min(4, c.NHKV - s * 4)
            cw = kh * 128
            base = (s % 2) * 4
            for b in range(nblk):
                nch = min(8, c.NCH - b * 8)
                buf, wv = stream_block(wk_d, b * 8 * 128, nch, s * 512, cw)
                for cc in range(nch):
                    ci = b * 8 + cc
                    for kk in range(kh):
                        bank = psum[base + kk]
                        mm(bank, bank.ap[:, 0:512],
                           wblk[buf][:, cc * cw + kk * 128:cc * cw + (kk + 1) * 128],
                           xnT[:, ci * 512:(ci + 1) * 512],
                           start=(ci == 0), stop=(ci == c.NCH - 1),
                           waits=[wv, ("V", xnT_blk_ready[b])])
                rg_wblk[buf].use("T", sch.cnt["T"])
            for kk in range(kh):
                k = s * 4 + kk
                bank = psum[base + kk]
                co = k * c.S + g * c.GROUP_T
                dst = kt[:, co:co + 512]
                a1 = sch.A([("T", bank.writer)], lambda e, d=dst, b=bank:
                           nc.scalar.activation(d, b.ap[:, 0:512], AF.Identity))
                bank.readers.append(("A", a1))
                sb = kk % 2
                kt_ready[k][g] = rope(dst, kt[64:128, co:co + 512], kt[0:64, co:co + 512],
                                      cosg[:, :], sing[:, :],
                                      qswp[sb][:, :], t1b[sb][:, :], sb, wtab, a1)

        for s in range((c.DKV + 511) // 512):
            cw = min(512, c.DKV - s * 512)
            base = (s % 2) * 4
            for b in range(nblk):
                nch = min(8, c.NCH - b * 8)
                buf, wv = stream_block(wv_d, b * 8 * 128, nch, s * 512, cw)
                for cc in range(nch):
                    ci = b * 8 + cc
                    for jl in range(c.GT):
                        bank = psum[base + jl]
                        mm(bank, bank.ap[:, 0:cw],
                           xnT[:, ci * 512 + jl * 128:ci * 512 + (jl + 1) * 128],
                           wblk[buf][:, cc * cw:(cc + 1) * cw],
                           start=(ci == 0), stop=(ci == c.NCH - 1),
                           waits=[wv, ("V", xnT_blk_ready[b])])
                rg_wblk[buf].use("T", sch.cnt["T"])
            for jl in range(c.GT):
                i = g * c.GT + jl
                bank = psum[base + jl]
                nh = cw // 128
                dst = vaug[:, :].rearrange("p (t k v) -> p t k v", k=c.NHKV, v=129)[
                    :, i:i + 1, s * 4:s * 4 + nh, 0:128]
                src = bank.ap[:, 0:cw].rearrange("p (k v) -> p k v", v=128)
                vv = sch.V([("T", bank.writer), ("V", vo)],
                           lambda e, d=dst, s2=src: nc.vector.tensor_copy(d, s2))
                bank.readers.append(("V", vv))
                for kk in range(nh):
                    vaug_ready[s * 4 + kk][i] = vv
        b_done_T = sch.cnt["T"]
        rg_hs[0].use("T", b_done_T)
        rg_hs[1].use("T", b_done_T)

        if g + 1 < c.NG:
            wt = rg_tab.take(("tab", g + 1))
            lastrope = ("V", sch.cnt["V"])
            sch.X([lastrope] + wt, lambda e, gg=g + 1: e.dma_start(
                cosg[:, :], cost_d[:, gg * 512:(gg + 1) * 512]))
            tb = sch.X([], lambda e, gg=g + 1: e.dma_start(
                sing[:, :], sint_d[:, gg * 512:(gg + 1) * 512]))
            rg_tab.use(*tb)

        steps = []
        for jl in range(c.GT):
            jg = g * c.GT + jl
            for k in range(c.NHKV):
                for i in range(jg + 1):
                    steps.append((jl, k, i, jg))
        attn_ready = [[0] * c.NHKV for _ in range(c.GT)]

        def emit_scores(idx):
            jl, k, i, jg = steps[idx]
            sb_ = psum[idx % 2]
            w = [("V", kt_ready[k][i // c.GT]), ("V", qsw_ready[k])] + sb_.readers
            sb_.readers = []
            sb_.writer = sch.T(w, lambda e, o=sb_.ap, k=k, i=i, jl=jl:
                               nc.tensor.matmul(
                                   o[:, 0:512],
                                   kt[:, k * c.S + i * 128:k * c.S + (i + 1) * 128],
                                   qt[:, (k * 4 + jl) * 512:(k * 4 + jl + 1) * 512],
                                   start=True, stop=True))

        emit_scores(0)
        pvidx = 0
        pvb = None
        for idx, (jl, k, i, jg) in enumerate(steps):
            epar = idx % 2
            sb_ = psum[epar]
            if i == 0:
                pvb = [psum[4 + 2 * (pvidx % 2)], psum[5 + 2 * (pvidx % 2)]]
                pvidx += 1
            we = rg_ebuf[epar].take(("e", sch.cnt["A"]))
            av = sch.A([("T", sb_.writer)] + we, lambda e, p=epar, b=sb_:
                       nc.scalar.activation(ebuf[p][:, :], b.ap[:, 0:512],
                                            AF.Exp, bias=cvals[:, 3:4],
                                            scale=cvals[:, 2:3]))
            sb_.readers.append(("A", av))
            eready = ("A", av)
            if i == jg:
                vv = sch.V([eready, tri_ready], lambda e, p=epar:
                           nc.vector.tensor_tensor(ebuf[p][:, :], ebuf[p][:, :],
                                                   tri[:, :], OP.mult))
                eready = ("V", vv)
            if idx + 1 < len(steps):
                emit_scores(idx + 1)
            for g4 in range(4):
                w2 = [eready, ("V", vaug_ready[k][i])]
                bank = pvb[g4 // 2]
                if i == 0 and g4 == 0:
                    w2 += pvb[0].readers + pvb[1].readers
                    pvb[0].readers = []
                    pvb[1].readers = []
                tv2 = sch.T(w2, lambda e, b=bank, p=epar, g4=g4, k=k, i=i, jg=jg:
                            nc.tensor.matmul(
                                b.ap[:, (g4 % 2) * 129:(g4 % 2) * 129 + 129],
                                ebuf[p][:, g4 * 128:(g4 + 1) * 128],
                                vaug[:, (i * c.NHKV + k) * 129:
                                     (i * c.NHKV + k) * 129 + 129],
                                start=(i == 0 and g4 % 2 == 0),
                                stop=(i == jg and g4 % 2 == 1)))
                rg_ebuf[epar].use("T", tv2)
                if i == jg:
                    bank.writer = tv2
            if i == jg:
                rv = 0
                for bb in range(2):
                    dn = pvb[bb].ap[:, 0:258].rearrange("p (g v) -> p g v", v=129)[
                        :, :, 128:129]
                    rv = sch.V([("T", pvb[0].writer), ("T", pvb[1].writer)],
                               lambda e, d=dn, bb=bb: nc.vector.reciprocal(
                                   arcp[:, bb * 2:bb * 2 + 2], d))
                    pvb[bb].readers.append(("V", rv))
                wa = rg_hs[jl // 2].take(("ao", g)) if k == 0 else []
                for g4 in range(4):
                    bank = pvb[g4 // 2]
                    hcol = (4 * k + g4) * 128
                    vv = sch.V([("T", bank.writer), ("V", rv)] + (wa if g4 == 0 else []),
                               lambda e, b=bank, g4=g4, jl=jl, hc=hcol:
                               nc.vector.tensor_scalar(
                                   attn_out[jl][:, hc:hc + 128],
                                   b.ap[:, (g4 % 2) * 129:(g4 % 2) * 129 + 128],
                                   arcp[:, g4:g4 + 1], 0.0, OP.mult, OP.add))
                    bank.readers.append(("V", vv))
                    rg_hs[jl // 2].use("V", vv)
                attn_ready[jl][k] = sch.cnt["V"]
        rg_qtar.use("T", sch.cnt["T"])

        aoT_blk_ready = [0] * nblk
        wq0 = rg_qtar.take(("aoT", g))
        for jl in range(c.GT):
            for cq, vv in transpose_tile(
                    attn_out[jl], aoT, jl,
                    lambda ci, jl=jl: [("V", attn_ready[jl][ci // 4])],
                    wq0 if jl == 0 else []):
                if jl == c.GT - 1 and (cq % 2 == 1 or cq == c.NCH // 4 - 1):
                    aoT_blk_ready[min(cq // 2, nblk - 1)] = vv
            rg_hs[jl // 2].use("T", sch.cnt["T"])
        for b in range(nblk):
            if aoT_blk_ready[b] == 0:
                aoT_blk_ready[b] = aoT_blk_ready[-1]
        xdma = [0] * c.GT
        hw0 = [[] for _ in range(c.GT)]
        for jl in range(c.GT):
            row0 = (g * c.GT + jl) * 128
            wts = rg_hs[jl].take(("h", g))
            if jl < 2:
                xdma[jl] = sch.X(wts, lambda e, r=row0, jl=jl:
                                 e.dma_start(h[jl][:, :], x_d[r:r + 128, :]))
            else:
                xdma[jl] = axv[jl]
                hw0[jl] = wts
        h_add = [[0] * (c.D // 512) for _ in range(c.GT)]
        for s in range(c.D // 512):
            base = (s % 2) * 4
            for b in range(nblk):
                nch = min(8, c.NCH - b * 8)
                buf, wv = stream_block(wo_d, b * 8 * 128, nch, s * 512, 512)
                for cc in range(nch):
                    ci = b * 8 + cc
                    for jl in range(c.GT):
                        bank = psum[base + jl]
                        mm(bank, bank.ap[:, 0:512],
                           aoT[:, ci * 512 + jl * 128:ci * 512 + (jl + 1) * 128],
                           wblk[buf][:, cc * 512:(cc + 1) * 512],
                           start=(ci == 0), stop=(ci == c.NCH - 1),
                           waits=[wv, ("V", aoT_blk_ready[b])])
                rg_wblk[buf].use("T", sch.cnt["T"])
            for jl in range(c.GT):
                bank = psum[base + jl]
                vv = sch.V([("T", bank.writer), xdma[jl]]
                           + (hw0[jl] if s == 0 else []),
                           lambda e, b=bank, jl=jl, s=s: nc.vector.tensor_tensor(
                               h[jl][:, s * 512:(s + 1) * 512], b.ap[:, 0:512],
                               h[jl][:, s * 512:(s + 1) * 512], OP.add))
                bank.readers.append(("V", vv))
                h_add[jl][s] = vv
                rg_hs[jl].use("V", vv)
        rg_qtar.use("T", sch.cnt["T"])

        hnT_blk_ready = [0] * nblk
        wq0 = rg_qtar.take(("hnT", g))
        for jl in range(c.GT):
            wxn = rg_xnhn.take(("E", g, jl))
            v1, a3 = layer_norm(h[jl], jl, [("V", h_add[jl][-1])], xnhn, wxn)
            rg_hs[jl].use("V", v1)
            rg_hs[jl].use("A", a3)
            for cq, vv in transpose_tile(xnhn, hnT, jl, lambda ci, a3=a3: [("A", a3)],
                                         wq0 if jl == 0 else []):
                if jl == c.GT - 1 and (cq % 2 == 1 or cq == c.NCH // 4 - 1):
                    hnT_blk_ready[min(cq // 2, nblk - 1)] = vv
            rg_xnhn.use("T", sch.cnt["T"])
        for b in range(nblk):
            if hnT_blk_ready[b] == 0:
                hnT_blk_ready[b] = hnT_blk_ready[-1]
        bank_m = psum[2]
        for ci in range(c.NCH):
            mm(bank_m, bank_m.ap[0:c.EW, 0:512],
               awr[:, ci * c.EW:(ci + 1) * c.EW], hnT[:, ci * 512:(ci + 1) * 512],
               start=(ci == 0), stop=(ci == c.NCH - 1),
               waits=[awr_ready, ("V", hnT_blk_ready[ci // 8])])
        rg_qtar.use("T", sch.cnt["T"])
        wml = rg_midlog.take(("ml", g))
        aml = sch.A([("T", bank_m.writer)] + wml, lambda e: nc.scalar.activation(
            midlog_bf[:, :], bank_m.ap[0:c.EW, 0:512], AF.Identity))
        bank_m.readers.append(("A", aml))
        bank_t = psum[3]
        for jl in range(c.GT):
            w = [("A", aml), iden_ready] + (bank_t.readers if jl == 0 else [])
            if jl == 0:
                bank_t.readers = []
            tv = sch.T(w, lambda e, jl=jl: nc.tensor.matmul(
                bank_t.ap[:, jl * c.EW:(jl + 1) * c.EW],
                midlog_bf[:, jl * 128:(jl + 1) * 128],
                iden[0:c.EW, 0:c.EW], start=True, stop=True))
            rg_midlog.use("T", tv)
            bank_t.writer = tv
        vml = sch.V([("T", bank_t.writer)], lambda e: nc.vector.tensor_copy(
            midlog_sb[:, 0:c.GT * c.EW], bank_t.ap[:, 0:c.GT * c.EW]))
        bank_t.readers.append(("V", vml))

        bank_gt = psum[4]
        for jl in range(c.GT):
            lg = midlog_sb[:, jl * c.EW + c.NE * c.R:(jl + 1) * c.EW]
            md = midlog_sb[:, jl * c.EW:jl * c.EW + c.NE * c.R]
            m1, m2 = gsc[:, 0:1], gsc[:, 1:2]
            eq, l2 = gsc[:, 4:4 + c.NE], gsc[:, 8:8 + c.NE]
            mask, em = gsc[:, 12:12 + c.NE], gsc[:, 16:16 + c.NE]
            g4v = gsc[:, 20:20 + c.NE]
            p = jl % 2
            c1 = sch.V([("V", vml)], lambda e, lg=lg, m1=m1:
                       nc.vector.reduce_max(m1, lg, AX, negate=True))
            c2 = sch.V([("V", c1)], lambda e, lg=lg, m1=m1, eq=eq:
                       nc.vector.tensor_scalar(
                           eq, lg, m1[:, 0:1], 0.0, OP.add, OP.is_equal))
            c3 = sch.V([("V", c2)], lambda e, lg=lg, eq=eq, l2=l2:
                       nc.vector.scalar_tensor_tensor(
                           l2, eq, -1e30, lg, OP.mult, OP.add))
            c4 = sch.V([("V", c3)], lambda e, l2=l2, m2=m2:
                       nc.vector.reduce_max(m2, l2, AX, negate=True))
            vmask = sch.V([("V", c4)], lambda e, lg=lg, m2=m2, mask=mask:
                          nc.vector.tensor_scalar(
                              mask, lg, m2[:, 0:1], 0.0, OP.add, OP.is_ge))
            aex = sch.A([("V", vmask)], lambda e, lg=lg, m1=m1, p=p:
                        nc.scalar.activation(gex[p][:, :], lg, AF.Exp, bias=m1[:, 0:1]))
            c5 = sch.V([("A", aex), ("V", vmask)], lambda e, em=em, p=p, mask=mask:
                       nc.vector.tensor_tensor(em, gex[p][:, :], mask, OP.mult))
            c6 = sch.V([("V", c5)], lambda e, em=em:
                       nc.vector.reduce_sum(gden[:, 0:1], em, AX))
            c7 = sch.V([("V", c6)], lambda e:
                       nc.vector.reciprocal(gden[:, 1:2], gden[:, 0:1]))
            vg = sch.V([("V", c7)], lambda e, em=em, g4v=g4v: nc.vector.tensor_scalar(
                g4v, em, gden[:, 1:2], 4.0, OP.mult, OP.mult))
            wgm = rg_gmid.take(("gm", g, jl))
            vgm = 0
            for e4 in range(c.NE):
                vgm = sch.V([("V", vg)] + (wgm if e4 == 0 else []),
                            lambda e, md=md, e4=e4, g4v=g4v: nc.vector.tensor_scalar(
                                gmid[:, e4 * c.R:(e4 + 1) * c.R],
                                md[:, e4 * c.R:(e4 + 1) * c.R],
                                g4v[:, e4:e4 + 1], 0.0, OP.mult, OP.add))
            w = [("V", vgm), iden_ready] + (bank_gt.readers if jl == 0 else [])
            if jl == 0:
                bank_gt.readers = []
            sch.T(w, lambda e, jl=jl: nc.tensor.matmul(
                bank_gt.ap[0:c.NE * c.R, jl * 128:(jl + 1) * 128],
                gmid[:, :], iden[:, :], start=True, stop=True))
            tv = sch.T([], lambda e, jl=jl: nc.tensor.matmul(
                bank_gt.ap[c.NE * c.R:2 * c.NE * c.R, jl * 128:(jl + 1) * 128],
                gmid[:, :], iden[:, :], start=True, stop=True))
            rg_gmid.use("T", tv)
            bank_gt.writer = tv
        wgt = rg_gmidT.take(("gt", g))
        vgt = sch.V([("T", bank_gt.writer)] + wgt, lambda e: nc.vector.tensor_copy(
            gmidT[:, 0:c.GT * 128], bank_gt.ap[:, 0:c.GT * 128]))
        bank_gt.readers.append(("V", vgt))

        for jl in range(c.GT):
            last_add = 0
            for sl in range(c.D // 512):
                bank = psum[5 + sl % 2]
                w = [("V", vgt), bcat_ready] + bank.readers
                bank.readers = []
                if sl < c.D // 1024:
                    rhs = bcat[0:64, sl * 512:(sl + 1) * 512]
                    lhs = gmidT[0:64, jl * 128:(jl + 1) * 128]
                else:
                    rhs = bcat[64:128, (sl - c.D // 1024) * 512:
                               (sl - c.D // 1024 + 1) * 512]
                    lhs = gmidT[64:128, jl * 128:(jl + 1) * 128]
                tv = sch.T(w, lambda e, b=bank, l=lhs, r=rhs: nc.tensor.matmul(
                    b.ap[:, 0:512], l, r, start=True, stop=True))
                bank.writer = tv
                rg_gmidT.use("T", tv)
                vv = sch.V([("T", tv), ("V", h_add[jl][sl])],
                           lambda e, b=bank, jl=jl, sl=sl:
                           nc.vector.tensor_tensor(
                               h[jl][:, sl * 512:(sl + 1) * 512], b.ap[:, 0:512],
                               h[jl][:, sl * 512:(sl + 1) * 512], OP.add))
                bank.readers.append(("V", vv))
                last_add = vv
                rg_hs[jl].use("V", vv)
            row0 = (g * c.GT + jl) * 128
            xv = sch.X([("V", last_add)], lambda e, r=row0, jl=jl:
                       e.dma_start(out_d[r:r + 128, :], h[jl][:, :]))
            rg_hs[jl].use(*xv)

    sch.raw("sync", [(k, sch.cnt[k]) for k in DMA_FAM["X"]])
    sch.raw("gpsimd", [(k, sch.cnt[k]) for k in DMA_FAM["W"]])
    sch.raw("scalar", [(k, sch.cnt[k]) for k in DMA_FAM["Z"]])

    with nc.Block() as block:
        @block.tensor
        def _(e):
            for f in sch.q["tensor"]:
                f(e)

        @block.scalar
        def _(e):
            for f in sch.q["scalar"]:
                f(e)

        @block.vector
        def _(e):
            for f in sch.q["vector"]:
                f(e)

        @block.sync
        def _(e):
            for f in sch.q["sync"]:
                f(e)

        @block.gpsimd
        def _(e):
            for f in sch.q["gpsimd"]:
                f(e)

    return nc


def make_tables(cfg):
    c = cfg
    ex = np.arange(0, 128, 2, dtype=np.float32) / np.float32(128.0)
    inv = (np.float32(1.0) / (np.float32(10000.0) ** ex)).astype(np.float32)
    t = np.arange(c.S, dtype=np.float32)
    fr = t[:, None] * inv[None, :]
    cosv = np.cos(fr).astype(np.float32)
    sinv = np.sin(fr).astype(np.float32)
    cost = np.concatenate([cosv.T, cosv.T], axis=0)
    sint = np.concatenate([-sinv.T, sinv.T], axis=0)
    iden = np.eye(128, dtype=np.float32).astype(BFNP)
    m = np.arange(128)[:, None]
    q = np.arange(128)[None, :]
    trib = (q >= m).astype(np.float32)
    tri = np.ascontiguousarray(np.concatenate([trib] * 4, axis=1)).astype(BFNP)
    return cost, sint, iden, tri


def make_inputs(cfg, x, wq, wk, wv, wo, A, B, Wr):
    c = cfg
    cost, sint, iden, tri = make_tables(c)
    awr = np.ascontiguousarray(np.concatenate(
        [np.ascontiguousarray(A.transpose(2, 0, 1)).reshape(c.D, c.NE * c.R),
         np.ascontiguousarray(Wr.T)], axis=1), dtype=np.float32)
    bc = np.ascontiguousarray(B.transpose(0, 2, 1)).reshape(c.NE * c.R, c.D)
    bcat = np.ascontiguousarray(
        np.concatenate([bc[:, :c.D // 2], bc[:, c.D // 2:]], axis=0), dtype=np.float32)
    return dict(x=np.ascontiguousarray(x, dtype=np.float32),
                wq=np.ascontiguousarray(wq, dtype=np.float32),
                wk=np.ascontiguousarray(wk, dtype=np.float32),
                wv=np.ascontiguousarray(wv, dtype=np.float32),
                wo=np.ascontiguousarray(wo, dtype=np.float32),
                awr=awr, bcat=bcat, cost=cost, sint=sint, iden=iden, tri=tri)


def kernel(L0_attn_x, L0_attn_Wq, L0_attn_Wk, L0_attn_Wv, L0_attn_Wo,
           L0_ffn_A, L0_ffn_B, L0_ffn_Wr):
    cfg = CFG(D=4096, S=2048, NHQ=32, NHKV=8)
    x = np.asarray(L0_attn_x, dtype=np.float32)
    nb = x.shape[0]
    base = make_inputs(cfg, x[0],
                       np.asarray(L0_attn_Wq), np.asarray(L0_attn_Wk),
                       np.asarray(L0_attn_Wv), np.asarray(L0_attn_Wo),
                       np.asarray(L0_ffn_A), np.asarray(L0_ffn_B),
                       np.asarray(L0_ffn_Wr))
    in_maps = []
    for i in range(nb):
        m = dict(base)
        m["x"] = np.ascontiguousarray(x[i])
        in_maps.append(m)
    nc = build_program(cfg)
    res = run_bass_kernel_spmd(nc, in_maps, core_ids=list(range(nb)))
    out = np.stack([np.asarray(res.results[i]["out"]) for i in range(nb)], axis=0)
    return out.astype(np.float32)


# version 04
# speedup vs baseline: 1.4586x; 1.0851x over previous
import numpy as np
import ml_dtypes
import concourse.bass as bass
import concourse.mybir as mybir
from concourse.bass_utils import run_bass_kernel_spmd

BFNP = ml_dtypes.bfloat16
F32 = mybir.dt.float32
BF16 = mybir.dt.bfloat16
AX = mybir.AxisListType.X
OP = mybir.AluOpType
AF = mybir.ActivationFunctionType

NEG_SHIFT = -16.0
EPS = 1e-5


class CFG:
    def __init__(self, D, S, NHQ, NHKV):
        self.D = D
        self.S = S
        self.NHQ = NHQ
        self.NHKV = NHKV
        self.GROUP_T = 512
        self.HD = 128
        self.NCH = D // 128
        self.GT = 4
        self.NG = S // 512
        self.NT = S // 128
        self.DKV = NHKV * 128
        self.NE = 4
        self.R = 16
        self.EW = self.NE * self.R + self.NE
        self.SCL = float(1.0 / np.sqrt(128.0))
        assert self.NHQ // self.NHKV == 4 and self.NHQ % 4 == 0
        assert self.D % 512 == 0 and self.S % 512 == 0


ENGS = ("tensor", "scalar", "vector", "sync", "gpsimd")


SEM_NAMES = ("T", "A", "V", "X0", "X1", "X2", "X3", "Z0", "Z1", "W0", "W1")
DMA_FAM = {"X": ("X0", "X1", "X2", "X3"), "Z": ("Z0", "Z1"), "W": ("W0", "W1")}


class Sched:
    def __init__(self):
        self.q = {k: [] for k in ENGS}
        self.cnt = {k: 0 for k in SEM_NAMES}
        self.rr = {k: 0 for k in DMA_FAM}
        self.waited = {k: {} for k in ENGS}
        self.semh = {}

    def op(self, eng, semkey, inc, waits, fn):
        self.cnt[semkey] += inc
        val = self.cnt[semkey]
        wl = {}
        for s, v in waits:
            if v and v > 0:
                wl[s] = max(wl.get(s, 0), v)
        wl = sorted(wl.items())

        def run(e, wl=wl, fn=fn, semkey=semkey, inc=inc, eng=eng):
            wd = self.waited[eng]
            for s, v in wl:
                if wd.get(s, 0) < v:
                    e.wait_ge(self.semh[s], v)
                    wd[s] = v
            ins = fn(e)
            if inc:
                ins.then_inc(self.semh[semkey], inc)

        self.q[eng].append(run)
        return val

    def T(self, waits, fn):
        return self.op("tensor", "T", 1, waits, fn)

    def A(self, waits, fn):
        return self.op("scalar", "A", 1, waits, fn)

    def V(self, waits, fn):
        return self.op("vector", "V", 1, waits, fn)

    def dma(self, eng, fam, waits, fn):
        k = DMA_FAM[fam][self.rr[fam] % len(DMA_FAM[fam])]
        self.rr[fam] += 1
        w = list(waits) + [(k, self.cnt[k])]
        v = self.op(eng, k, 16, w, fn)
        return (k, v)

    def X(self, waits, fn):
        return self.dma("sync", "X", waits, fn)

    def Z(self, waits, fn):
        return self.dma("scalar", "Z", waits, fn)

    def W(self, waits, fn):
        return self.dma("gpsimd", "W", waits, fn)

    def raw(self, eng, waits):
        wl = {}
        for s, v in waits:
            if v and v > 0:
                wl[s] = max(wl.get(s, 0), v)
        wl = sorted(wl.items())

        def run(e, wl=wl, eng=eng):
            wd = self.waited[eng]
            for s, v in wl:
                if wd.get(s, 0) < v:
                    e.wait_ge(self.semh[s], v)
                    wd[s] = v

        self.q[eng].append(run)


class Region:

    def __init__(self):
        self.users = []
        self.phase = None

    def use(self, sem, val):
        self.users.append((sem, val))

    def take(self, phase):
        if self.phase == phase:
            return []
        self.phase = phase
        u = self.users
        self.users = []
        return u


class Bank:
    def __init__(self, ap):
        self.ap = ap
        self.writer = 0
        self.readers = []


def build_program(cfg):
    c = cfg
    nc = bass.Bass("TRN2", target_bir_lowering=False)

    x_d = nc.dram_tensor("x", [c.S, c.D], F32, kind="ExternalInput").ap()
    wq_d = nc.dram_tensor("wq", [c.D, c.D], F32, kind="ExternalInput").ap()
    wk_d = nc.dram_tensor("wk", [c.D, c.DKV], F32, kind="ExternalInput").ap()
    wv_d = nc.dram_tensor("wv", [c.D, c.DKV], F32, kind="ExternalInput").ap()
    wo_d = nc.dram_tensor("wo", [c.D, c.D], F32, kind="ExternalInput").ap()
    awr_d = nc.dram_tensor("awr", [c.D, c.EW], F32, kind="ExternalInput").ap()
    bcat_d = nc.dram_tensor("bcat", [128, c.D // 2], F32, kind="ExternalInput").ap()
    cost_d = nc.dram_tensor("cost", [128, c.S], F32, kind="ExternalInput").ap()
    sint_d = nc.dram_tensor("sint", [128, c.S], F32, kind="ExternalInput").ap()
    iden_d = nc.dram_tensor("iden", [128, 128], BF16, kind="ExternalInput").ap()
    tri_d = nc.dram_tensor("tri", [128, 512], BF16, kind="ExternalInput").ap()
    out_d = nc.dram_tensor("out", [c.S, c.D], F32, kind="ExternalOutput").ap()
    nblk_ = (c.NCH + 7) // 8
    NBLK = nblk_ * (c.NHQ // 4 + (c.NHKV + 3) // 4 + (c.DKV + 511) // 512 + c.D // 512)
    wscr_d = nc.dram_tensor("wscr", [NBLK, 128, 8 * 512], BF16, kind="Internal").ap()

    sch = Sched()

    cursor = [0]
    backing = nc.sbuf_tensor("backing", [128, 212800], mybir.dt.uint8)
    backing_h = backing.__enter__()
    base_addr = nc.lookup_mloc(backing_h).addr

    def at(name, shape, dtype, off=None):
        per_part = int(np.prod(shape[1:])) * (4 if dtype == F32 else 2)
        if off is None:
            off = cursor[0]
            cursor[0] = off + ((per_part + 31) // 32) * 32
        assert off + per_part <= 212800
        return nc.alloc_sbuf_tensor_at(name, shape, dtype, offset=base_addr + off)

    def place(nbytes):
        off = cursor[0]
        cursor[0] += ((nbytes + 31) // 32) * 32
        return off

    TLB = c.D * 4
    kt = at("kt", [128, c.NHKV * c.S], BF16)
    vaug = at("vaug", [128, c.NT * c.NHKV * 129], BF16)
    arena1_off = place(2 * TLB)
    qtar_off = place(c.NHKV * 4 * c.GROUP_T * 2)
    hextra_off = place(2 * TLB)

    h = [at("h0", [128, c.D], F32, arena1_off),
         at("h1", [128, c.D], F32, arena1_off + TLB),
         at("h2", [128, c.D], F32, hextra_off),
         at("h3", [128, c.D], F32, hextra_off + TLB)]
    attn_out = [at(f"ao{j}", [128, c.D], BF16, arena1_off + j * c.D * 2)
                for j in range(c.GT)]
    xnT = at("xnT", [128, c.NCH * c.GROUP_T], BF16, arena1_off)
    qt = at("qt", [128, c.NHKV * 4 * c.GROUP_T], BF16, qtar_off)
    aoT = at("aoT", [128, c.NCH * c.GROUP_T], BF16, qtar_off)
    hnT = at("hnT", [128, c.NCH * c.GROUP_T], BF16, qtar_off)

    wblk = [at(f"wblk{i}", [128, 8 * 512], BF16) for i in range(2)]
    xnhn = at("xnhn", [128, c.D], BF16)
    rope_off = cursor[0]
    qswp = [at(f"qswp{i}", [128, 512], BF16) for i in range(2)]
    t1b = [at(f"t1b{i}", [128, 512], BF16) for i in range(2)]
    epair = [at("epair0", [128, 1024], BF16, rope_off),
             at("epair1", [128, 1024], BF16, rope_off + 2048)]
    cosg = at("cosg", [128, 512], F32)
    sing = at("sing", [128, 512], F32)
    awr = at("awr", [128, c.NCH * c.EW], BF16)
    bcat = at("bcat", [128, c.D // 2], BF16)
    iden = at("iden", [128, 128], BF16)
    tri = at("tri", [128, 512], BF16)
    midlog_bf = at("midlogbf", [c.EW, c.GROUP_T], BF16)
    midlog_sb = at("midlogsb", [128, c.GT * c.EW], F32)
    gmid = at("gmid", [128, c.NE * c.R], BF16)
    gmidT = at("gmidT", [128, c.GT * 128], BF16)
    bst = at("bst", [128, 6 * (c.D // 512)], F32)
    agg = [at(f"agg{i}", [128, 2], F32) for i in range(2)]
    lnv = [at(f"lnv{i}", [128, 1], F32) for i in range(2)]
    rstd = [at(f"rstd{i}", [128, 1], F32) for i in range(2)]
    negmr = [at(f"negmr{i}", [128, 1], F32) for i in range(2)]
    gsc = at("gsc", [128, 24], F32)
    gex = [at(f"gex{i}", [128, c.NE], F32) for i in range(2)]
    gden = at("gden", [128, 2], F32)
    arcp = at("arcp", [128, 4], F32)
    cvals = at("cvals", [128, 4], F32)

    pdh = []
    psum = []
    for i in range(4):
        cm = nc.psum_tensor(f"pd{i}", [128, 1024], F32)
        pdh.append(cm.__enter__())
    for i in range(8):
        psum.append(Bank(pdh[i // 2][:, (i % 2) * 512:(i % 2 + 1) * 512]))

    for s in SEM_NAMES:
        cm = nc.semaphore(f"sem{s}")
        sch.semh[s] = cm.__enter__()

    rg_qtar = Region()
    rg_hs = [Region() for _ in range(4)]
    rg_xnhn = Region()
    rg_wblk = [Region() for _ in range(2)]
    rg_qswp = [Region() for _ in range(2)]
    rg_ebuf = [Region() for _ in range(2)]
    rg_ropescr = Region()
    rg_tab = Region()
    rg_stats = [Region() for _ in range(2)]
    rg_midlog = Region()
    rg_gmid = Region()
    rg_gmidT = Region()

    blkid = [0]
    gblk = [0]
    cur_g = [0]
    bst_last = [0]
    ts_last = [0]
    gate_last = [0]
    msb_users = [[]]
    scratch_ready = [None] * NBLK

    def stream_block(dram, r0, nch, c0, cw):
        buf = blkid[0] % 2
        blkid[0] += 1
        bidx = gblk[0]
        gblk[0] += 1
        waits = rg_wblk[buf].take(("blk", blkid[0]))
        ncols = nch * cw
        if cur_g[0] == 0:
            src = dram[r0:r0 + nch * 128, c0:c0 + cw].rearrange("(a p) v -> p a v", p=128)
            dst = wblk[buf][:, 0:ncols].rearrange("p (a v) -> p a v", v=cw)
            wv = sch.W(waits, lambda e, dst=dst, src=src: e.dma_start(dst, src))
            if c.NG > 1:
                sv = sch.X([wv], lambda e, bidx=bidx, buf=buf, ncols=ncols:
                           e.dma_start(wscr_d[bidx:bidx + 1, :, 0:ncols],
                                       wblk[buf][:, 0:ncols]))
                scratch_ready[bidx] = sv
                rg_wblk[buf].use(*sv)
        else:
            wv = sch.X(waits + [scratch_ready[bidx]],
                       lambda e, bidx=bidx, buf=buf, ncols=ncols:
                       e.dma_start(wblk[buf][:, 0:ncols],
                                   wscr_d[bidx:bidx + 1, :, 0:ncols]))
        return buf, wv

    def mm(bank, out_ap, lhsT, rhs, start, stop, waits):
        w = list(waits)
        if start:
            w += bank.readers
            bank.readers = []
        tv = sch.T(w, lambda e, o=out_ap, l=lhsT, r=rhs, st=start, sp=stop:
                   nc.tensor.matmul(o, l, r, start=st, stop=sp))
        if stop:
            bank.writer = tv
        return tv

    def evac_quad(bank, dstT, cq, jl, extra_waits):
        src = bank.ap[:, 0:512].rearrange("p (a q) -> p a q", q=128)
        dst = dstT[:, cq * 4 * c.GROUP_T:(cq * 4 + 4) * c.GROUP_T] \
            .rearrange("p (a t) -> p a t", t=c.GROUP_T)[:, :, jl * 128:(jl + 1) * 128]
        vv = sch.V([("T", bank.writer)] + extra_waits,
                   lambda e, d=dst, s=src: nc.vector.tensor_copy(d, s))
        bank.readers.append(("V", vv))
        return vv

    def transpose_tile(src_bf, dstT, jl, ready_fn, first_evac_waits):
        res = []
        for cq in range(c.NCH // 4):
            bank = psum[cq % 2]
            tv = 0
            for ciq in range(4):
                ci = cq * 4 + ciq
                w = ready_fn(ci)
                if ciq == 0:
                    w = w + bank.readers
                    bank.readers = []
                tv = sch.T(w, lambda e, b=bank, ciq=ciq, ci=ci, s=src_bf:
                           nc.tensor.matmul(b.ap[:, ciq * 128:(ciq + 1) * 128],
                                            s[:, ci * 128:(ci + 1) * 128], iden[:, :],
                                            start=True, stop=True))
            bank.writer = tv
            ew = first_evac_waits if cq == 0 else []
            res.append((cq, evac_quad(bank, dstT, cq, jl, ew)))
        return res

    def layer_norm(xt, jl, extra_waits, dst_bf, dst_waits):
        p = jl % 2
        wst = rg_stats[p].take(("ln", sch.cnt["V"]))
        v1 = 0
        for su in range(c.D // 512):
            v1 = sch.V((extra_waits + wst + [("V", bst_last[0])]) if su == 0 else [],
                       lambda e, s=xt, su=su: nc.vector.bn_stats(
                           bst[:, 6 * su:6 * (su + 1)], s[:, 512 * su:512 * (su + 1)]))
        v2 = sch.V([("V", v1)], lambda e, p=p: nc.vector.bn_aggr(agg[p][:, :], bst[:, :]))
        bst_last[0] = v2
        a1 = sch.A([("V", v2)] + wst, lambda e, p=p: nc.scalar.activation(
            lnv[p][:, :], agg[p][:, 1:2], AF.Ln, bias=cvals[:, 0:1]))
        a2 = sch.A([("A", a1)], lambda e, p=p: nc.scalar.activation(
            rstd[p][:, :], lnv[p][:, :], AF.Exp, scale=cvals[:, 1:2]))
        v3 = sch.V([("A", a2), ("V", v2)], lambda e, p=p: nc.vector.tensor_scalar(
            negmr[p][:, :], agg[p][:, 0:1], rstd[p][:, 0:1], -1.0, OP.mult, OP.mult))
        rg_stats[p].use("V", v3)
        a3s = []
        for ch in range(max(1, c.D // 1024)):
            cw_ = min(1024, c.D)
            a3 = sch.A(([("V", v3), ("A", a2)] + dst_waits) if ch == 0 else [],
                       lambda e, s=xt, d=dst_bf, p=p, ch=ch, cw_=cw_:
                       nc.scalar.activation(d[:, ch * cw_:(ch + 1) * cw_],
                                            s[:, ch * cw_:(ch + 1) * cw_], AF.Identity,
                                            bias=negmr[p][:, 0:1], scale=rstd[p][:, 0:1]))
            a3s.append(a3)
        rg_stats[p].use("A", a3s[-1])
        return v1, a3s

    def rope(dst_ap, swap_hi, swap_lo, cos_ap, sin_ap, sw_ap, t1_ap, sb, waits, prod):
        wq0 = rg_qswp[sb].take(("rp", sch.rr["Z"])) + rg_ropescr.take(("B", cur_g[0]))
        z1 = sch.Z([("A", prod)] + wq0 + waits,
                   lambda e, s=swap_hi, sb=sb: e.dma_start(qswp[sb][0:64, :], s))
        z2 = sch.Z([("A", prod)] + wq0,
                   lambda e, s=swap_lo, sb=sb: e.dma_start(qswp[sb][64:128, :], s))
        rg_ropescr.use(*z1)
        rg_ropescr.use(*z2)
        v1 = sch.V([z1, z2, ("A", prod)] + wq0,
                   lambda e, t=t1_ap, sw=sw_ap, s=sin_ap:
                   nc.vector.tensor_tensor(t, sw, s, OP.mult))
        v2 = sch.V([("V", v1)], lambda e, sw=sw_ap, d=dst_ap, co=cos_ap:
                   nc.vector.tensor_tensor(sw, d, co, OP.mult))
        v3 = sch.V([("V", v2)], lambda e, d=dst_ap, t=t1_ap, sw=sw_ap:
                   nc.vector.tensor_tensor(d, t, sw, OP.add))
        rg_qswp[sb].use("V", v3)
        rg_ropescr.use("V", v3)
        return v3

    def qt_head(k, g4, p0=0, p1=128):
        return qt[p0:p1, :].rearrange("p (k j q) -> p k j q", k=c.NHKV, q=512)[
            :, k:k + 1, :, g4 * 128:(g4 + 1) * 128]

    def four_dim(ap2d):
        return ap2d.rearrange("p (k j q) -> p k j q", k=1, q=128)

    xc1 = sch.X([], lambda e: e.dma_start(iden[:, :], iden_d[:, :]))
    xc2 = sch.X([], lambda e: e.dma_start(tri[:, :], tri_d[:, :]))
    asrc = awr_d[:, :].rearrange("(a p) v -> p a v", p=128)
    adst = awr[:, :].rearrange("p (a v) -> p a v", v=c.EW)
    wc1 = sch.W([], lambda e, d=adst, s=asrc: e.dma_start(d, s))
    wc2 = sch.W([], lambda e: e.dma_start(bcat[:, :], bcat_d[:, :]))
    ones_ap = vaug[:, :].rearrange("p (s v) -> p s v", v=129)[:, :, 128:129]
    vo = sch.V([], lambda e, a=ones_ap: nc.vector.memset(a, 1.0))
    sch.V([], lambda e: nc.vector.memset(cvals[:, 0:1], EPS))
    sch.V([], lambda e: nc.vector.memset(cvals[:, 1:2], -0.5))
    sch.V([], lambda e: nc.vector.memset(cvals[:, 2:3], c.SCL))
    cvr = sch.V([], lambda e: nc.vector.memset(cvals[:, 3:4], NEG_SHIFT))
    sch.raw("scalar", [("V", cvr)])
    tb = sch.X([], lambda e: e.dma_start(cosg[:, :], cost_d[:, 0:512]))
    tb = sch.X([], lambda e: e.dma_start(sing[:, :], sint_d[:, 0:512]))
    rg_tab.use(*tb)
    iden_ready = xc1
    tri_ready = xc2
    awr_ready = wc1
    bcat_ready = wc2

    kt_ready = [[0] * c.NG for _ in range(c.NHKV)]
    vaug_ready = [[0] * c.NT for _ in range(c.NHKV)]
    qsw_ready = [0] * c.NHKV

    for g in range(c.NG):
        cur_g[0] = g
        gblk[0] = 0
        nblk = (c.NCH + 7) // 8
        xnT_blk_ready = [0] * nblk
        axv = [0] * c.GT
        for jl in range(c.GT):
            row0 = (g * c.GT + jl) * 128
            slot = 2 + (jl % 2)
            wts = rg_hs[slot].take(("Ax", g, jl))
            xv = sch.X(wts, lambda e, r=row0, s=slot: e.dma_start(h[s][:, :],
                                                                  x_d[r:r + 128, :]))
            axv[jl] = xv
            rg_hs[slot].use(*xv)
            wxn = rg_xnhn.take(("A", g, jl))
            v1, a3s = layer_norm(h[slot], jl, [xv], xnhn, wxn)
            rg_hs[slot].use("V", v1)
            rg_hs[slot].use("A", a3s[-1])
            warena = (rg_hs[0].take(("xnT", g)) + rg_hs[1].take(("xnT", g))) if jl == 0 else []
            for cq, vv in transpose_tile(xnhn, xnT, jl,
                                         lambda ci, a3s=a3s: [("A", a3s[ci // 8])],
                                         warena):
                if jl == c.GT - 1 and (cq % 2 == 1 or cq == c.NCH // 4 - 1):
                    xnT_blk_ready[min(cq // 2, nblk - 1)] = vv
            rg_xnhn.use("T", sch.cnt["T"])
        for b in range(nblk):
            if xnT_blk_ready[b] == 0:
                xnT_blk_ready[b] = xnT_blk_ready[-1]

        wtab = [tb]
        for s in range(c.NHQ // 4):
            base = (s % 2) * 4
            for b in range(nblk):
                nch = min(8, c.NCH - b * 8)
                buf, wv = stream_block(wq_d, b * 8 * 128, nch, s * 512, 512)
                for cc in range(nch):
                    ci = b * 8 + cc
                    for hh in range(4):
                        bank = psum[base + hh]
                        mm(bank, bank.ap[:, 0:512],
                           wblk[buf][:, cc * 512 + hh * 128:cc * 512 + (hh + 1) * 128],
                           xnT[:, ci * 512:(ci + 1) * 512],
                           start=(ci == 0), stop=(ci == c.NCH - 1),
                           waits=[wv, ("V", xnT_blk_ready[b])])
                rg_wblk[buf].use("T", sch.cnt["T"])
            for hh in range(4):
                bank = psum[base + hh]
                dst = qt_head(s, hh)
                wq0 = rg_qtar.take(("qt", g)) if (s == 0 and hh == 0) else []
                a1 = sch.A([("T", bank.writer)] + wq0,
                           lambda e, d=dst, b=bank: nc.scalar.activation(
                               d, b.ap[:, 0:512], AF.Identity))
                bank.readers.append(("A", a1))
                sb = hh % 2
                rope(dst, qt_head(s, hh, 64, 128), qt_head(s, hh, 0, 64),
                     four_dim(cosg[:, :]), four_dim(sing[:, :]),
                     four_dim(qswp[sb][:, :]), four_dim(t1b[sb][:, :]),
                     sb, wtab, a1)
            qsw_ready[s] = sch.cnt["V"]

        for s in range((c.NHKV + 3) // 4):
            kh = min(4, c.NHKV - s * 4)
            cw = kh * 128
            base = (s % 2) * 4
            for b in range(nblk):
                nch = min(8, c.NCH - b * 8)
                buf, wv = stream_block(wk_d, b * 8 * 128, nch, s * 512, cw)
                for cc in range(nch):
                    ci = b * 8 + cc
                    for kk in range(kh):
                        bank = psum[base + kk]
                        mm(bank, bank.ap[:, 0:512],
                           wblk[buf][:, cc * cw + kk * 128:cc * cw + (kk + 1) * 128],
                           xnT[:, ci * 512:(ci + 1) * 512],
                           start=(ci == 0), stop=(ci == c.NCH - 1),
                           waits=[wv, ("V", xnT_blk_ready[b])])
                rg_wblk[buf].use("T", sch.cnt["T"])
            for kk in range(kh):
                k = s * 4 + kk
                bank = psum[base + kk]
                co = k * c.S + g * c.GROUP_T
                dst = kt[:, co:co + 512]
                a1 = sch.A([("T", bank.writer)], lambda e, d=dst, b=bank:
                           nc.scalar.activation(d, b.ap[:, 0:512], AF.Identity))
                bank.readers.append(("A", a1))
                sb = kk % 2
                kt_ready[k][g] = rope(dst, kt[64:128, co:co + 512], kt[0:64, co:co + 512],
                                      cosg[:, :], sing[:, :],
                                      qswp[sb][:, :], t1b[sb][:, :], sb, wtab, a1)

        for s in range((c.DKV + 511) // 512):
            cw = min(512, c.DKV - s * 512)
            base = (s % 2) * 4
            for b in range(nblk):
                nch = min(8, c.NCH - b * 8)
                buf, wv = stream_block(wv_d, b * 8 * 128, nch, s * 512, cw)
                for cc in range(nch):
                    ci = b * 8 + cc
                    for jl in range(c.GT):
                        bank = psum[base + jl]
                        mm(bank, bank.ap[:, 0:cw],
                           xnT[:, ci * 512 + jl * 128:ci * 512 + (jl + 1) * 128],
                           wblk[buf][:, cc * cw:(cc + 1) * cw],
                           start=(ci == 0), stop=(ci == c.NCH - 1),
                           waits=[wv, ("V", xnT_blk_ready[b])])
                rg_wblk[buf].use("T", sch.cnt["T"])
            for jl in range(c.GT):
                i = g * c.GT + jl
                bank = psum[base + jl]
                nh = cw // 128
                dst = vaug[:, :].rearrange("p (t k v) -> p t k v", k=c.NHKV, v=129)[
                    :, i:i + 1, s * 4:s * 4 + nh, 0:128]
                src = bank.ap[:, 0:cw].rearrange("p (k v) -> p k v", v=128)
                vv = sch.V([("T", bank.writer), ("V", vo)],
                           lambda e, d=dst, s2=src: nc.vector.tensor_copy(d, s2))
                bank.readers.append(("V", vv))
                for kk in range(nh):
                    vaug_ready[s * 4 + kk][i] = vv
        b_done_T = sch.cnt["T"]
        rg_hs[0].use("T", b_done_T)
        rg_hs[1].use("T", b_done_T)

        if g + 1 < c.NG:
            wt = rg_tab.take(("tab", g + 1))
            lastrope = ("V", sch.cnt["V"])
            sch.X([lastrope] + wt, lambda e, gg=g + 1: e.dma_start(
                cosg[:, :], cost_d[:, gg * 512:(gg + 1) * 512]))
            tb = sch.X([], lambda e, gg=g + 1: e.dma_start(
                sing[:, :], sint_d[:, gg * 512:(gg + 1) * 512]))
            rg_tab.use(*tb)

        steps = []
        for jl in range(c.GT):
            jg = g * c.GT + jl
            for k in range(c.NHKV):
                for i in range(jg + 1):
                    steps.append((jl, k, i, jg))
        NSTEP = len(steps)
        attn_ready = [[0] * c.NHKV for _ in range(c.GT)]

        def emit_scores(idx):
            jl, k, i, jg = steps[idx]
            sb_ = psum[idx % 4]
            w = [("V", kt_ready[k][i // c.GT]), ("V", qsw_ready[k])] + sb_.readers
            sb_.readers = []
            sb_.writer = sch.T(w, lambda e, o=sb_.ap, k=k, i=i, jl=jl:
                               nc.tensor.matmul(
                                   o[:, 0:512],
                                   kt[:, k * c.S + i * 128:k * c.S + (i + 1) * 128],
                                   qt[:, (k * 4 + jl) * 512:(k * 4 + jl + 1) * 512],
                                   start=True, stop=True))

        emit_scores(0)
        if NSTEP > 1:
            emit_scores(1)
        pvidx = 0
        pvb = None
        first_exp = True
        for pidx in range(0, NSTEP, 2):
            mem = [ix for ix in (pidx, pidx + 1) if ix < NSTEP]
            p = (pidx // 2) % 2
            banks = [psum[ix % 4] for ix in mem]
            we = rg_ebuf[p].take(("e", sch.cnt["A"]))
            if first_exp:
                we = we + rg_ropescr.take(("C", g))
                first_exp = False
            ncols = 512 * len(mem)
            src = pdh[(pidx % 4) // 2][:, 0:ncols]
            wsc = [("T", b.writer) for b in banks]
            av = sch.A(wsc + we, lambda e, p=p, src=src, ncols=ncols:
                       nc.scalar.activation(epair[p][:, 0:ncols], src,
                                            AF.Exp, bias=cvals[:, 3:4],
                                            scale=cvals[:, 2:3]))
            rg_ropescr.use("A", av)
            for b in banks:
                b.readers.append(("A", av))
            eready = {}
            for mi, ix in enumerate(mem):
                jl, k, i, jg = steps[ix]
                er = ("A", av)
                if i == jg:
                    vv = sch.V([er, tri_ready], lambda e, p=p, mi=mi:
                               nc.vector.tensor_tensor(
                                   epair[p][:, mi * 512:(mi + 1) * 512],
                                   epair[p][:, mi * 512:(mi + 1) * 512],
                                   tri[:, :], OP.mult))
                    rg_ropescr.use("V", vv)
                    er = ("V", vv)
                eready[ix] = er
            for nx in (pidx + 2, pidx + 3):
                if nx < NSTEP:
                    emit_scores(nx)
            for mi, ix in enumerate(mem):
                jl, k, i, jg = steps[ix]
                if i == 0:
                    pvb = [psum[4 + 2 * (pvidx % 2)], psum[5 + 2 * (pvidx % 2)]]
                    pvidx += 1
                for g4 in range(4):
                    w2 = [eready[ix], ("V", vaug_ready[k][i])]
                    bank = pvb[g4 // 2]
                    if i == 0 and g4 == 0:
                        w2 += pvb[0].readers + pvb[1].readers
                        pvb[0].readers = []
                        pvb[1].readers = []
                    tv2 = sch.T(w2, lambda e, b=bank, p=p, mi=mi, g4=g4, k=k, i=i, jg=jg:
                                nc.tensor.matmul(
                                    b.ap[:, (g4 % 2) * 129:(g4 % 2) * 129 + 129],
                                    epair[p][:, mi * 512 + g4 * 128:
                                             mi * 512 + (g4 + 1) * 128],
                                    vaug[:, (i * c.NHKV + k) * 129:
                                         (i * c.NHKV + k) * 129 + 129],
                                    start=(i == 0 and g4 % 2 == 0),
                                    stop=(i == jg and g4 % 2 == 1)))
                    rg_ebuf[p].use("T", tv2)
                    rg_ropescr.use("T", tv2)
                    if i == jg:
                        bank.writer = tv2
                if i == jg:
                    rv = 0
                    for bb in range(2):
                        dn = pvb[bb].ap[:, 0:258].rearrange("p (g v) -> p g v", v=129)[
                            :, :, 128:129]
                        rv = sch.V([("T", pvb[0].writer), ("T", pvb[1].writer),
                                    ("V", ts_last[0])],
                                   lambda e, d=dn, bb=bb: nc.vector.reciprocal(
                                       arcp[:, bb * 2:bb * 2 + 2], d))
                        pvb[bb].readers.append(("V", rv))
                    wa = rg_hs[jl // 2].take(("ao", g)) if k == 0 else []
                    for g4 in range(4):
                        bank = pvb[g4 // 2]
                        hcol = (4 * k + g4) * 128
                        vv = sch.V([("T", bank.writer), ("V", rv)]
                                   + (wa if g4 == 0 else []),
                                   lambda e, b=bank, g4=g4, jl=jl, hc=hcol:
                                   nc.vector.tensor_scalar(
                                       attn_out[jl][:, hc:hc + 128],
                                       b.ap[:, (g4 % 2) * 129:(g4 % 2) * 129 + 128],
                                       arcp[:, g4:g4 + 1], 0.0, OP.mult, OP.add))
                        bank.readers.append(("V", vv))
                        rg_hs[jl // 2].use("V", vv)
                        ts_last[0] = vv
                    attn_ready[jl][k] = sch.cnt["V"]
        rg_qtar.use("T", sch.cnt["T"])

        aoT_blk_ready = [0] * nblk
        wq0 = rg_qtar.take(("aoT", g))
        for jl in range(c.GT):
            for cq, vv in transpose_tile(
                    attn_out[jl], aoT, jl,
                    lambda ci, jl=jl: [("V", attn_ready[jl][ci // 4])],
                    wq0 if jl == 0 else []):
                if jl == c.GT - 1 and (cq % 2 == 1 or cq == c.NCH // 4 - 1):
                    aoT_blk_ready[min(cq // 2, nblk - 1)] = vv
            rg_hs[jl // 2].use("T", sch.cnt["T"])
        for b in range(nblk):
            if aoT_blk_ready[b] == 0:
                aoT_blk_ready[b] = aoT_blk_ready[-1]
        xdma = [0] * c.GT
        hw0 = [[] for _ in range(c.GT)]
        for jl in range(c.GT):
            row0 = (g * c.GT + jl) * 128
            wts = rg_hs[jl].take(("h", g))
            if jl < 2:
                xdma[jl] = sch.X(wts, lambda e, r=row0, jl=jl:
                                 e.dma_start(h[jl][:, :], x_d[r:r + 128, :]))
            else:
                xdma[jl] = axv[jl]
                hw0[jl] = wts
        h_add = [[0] * (c.D // 512) for _ in range(c.GT)]
        for s in range(c.D // 512):
            base = (s % 2) * 4
            for b in range(nblk):
                nch = min(8, c.NCH - b * 8)
                buf, wv = stream_block(wo_d, b * 8 * 128, nch, s * 512, 512)
                for cc in range(nch):
                    ci = b * 8 + cc
                    for jl in range(c.GT):
                        bank = psum[base + jl]
                        mm(bank, bank.ap[:, 0:512],
                           aoT[:, ci * 512 + jl * 128:ci * 512 + (jl + 1) * 128],
                           wblk[buf][:, cc * 512:(cc + 1) * 512],
                           start=(ci == 0), stop=(ci == c.NCH - 1),
                           waits=[wv, ("V", aoT_blk_ready[b])])
                rg_wblk[buf].use("T", sch.cnt["T"])
            for jl in range(c.GT):
                bank = psum[base + jl]
                vv = sch.V([("T", bank.writer), xdma[jl]]
                           + (hw0[jl] if s == 0 else []),
                           lambda e, b=bank, jl=jl, s=s: nc.vector.tensor_tensor(
                               h[jl][:, s * 512:(s + 1) * 512], b.ap[:, 0:512],
                               h[jl][:, s * 512:(s + 1) * 512], OP.add))
                bank.readers.append(("V", vv))
                h_add[jl][s] = vv
                rg_hs[jl].use("V", vv)
        rg_qtar.use("T", sch.cnt["T"])

        hnT_blk_ready = [0] * nblk
        wq0 = rg_qtar.take(("hnT", g))
        for jl in range(c.GT):
            wxn = rg_xnhn.take(("E", g, jl))
            v1, a3s = layer_norm(h[jl], jl, [("V", h_add[jl][-1])], xnhn, wxn)
            rg_hs[jl].use("V", v1)
            rg_hs[jl].use("A", a3s[-1])
            for cq, vv in transpose_tile(xnhn, hnT, jl,
                                         lambda ci, a3s=a3s: [("A", a3s[ci // 8])],
                                         wq0 if jl == 0 else []):
                if jl == c.GT - 1 and (cq % 2 == 1 or cq == c.NCH // 4 - 1):
                    hnT_blk_ready[min(cq // 2, nblk - 1)] = vv
            rg_xnhn.use("T", sch.cnt["T"])
        for b in range(nblk):
            if hnT_blk_ready[b] == 0:
                hnT_blk_ready[b] = hnT_blk_ready[-1]
        bank_m = psum[2]
        for ci in range(c.NCH):
            mm(bank_m, bank_m.ap[0:c.EW, 0:512],
               awr[:, ci * c.EW:(ci + 1) * c.EW], hnT[:, ci * 512:(ci + 1) * 512],
               start=(ci == 0), stop=(ci == c.NCH - 1),
               waits=[awr_ready, ("V", hnT_blk_ready[ci // 8])])
        rg_qtar.use("T", sch.cnt["T"])
        wml = rg_midlog.take(("ml", g))
        aml = sch.A([("T", bank_m.writer)] + wml, lambda e: nc.scalar.activation(
            midlog_bf[:, :], bank_m.ap[0:c.EW, 0:512], AF.Identity))
        bank_m.readers.append(("A", aml))
        bank_t = psum[3]
        for jl in range(c.GT):
            w = [("A", aml), iden_ready] + (bank_t.readers if jl == 0 else [])
            if jl == 0:
                bank_t.readers = []
            tv = sch.T(w, lambda e, jl=jl: nc.tensor.matmul(
                bank_t.ap[:, jl * c.EW:(jl + 1) * c.EW],
                midlog_bf[:, jl * 128:(jl + 1) * 128],
                iden[0:c.EW, 0:c.EW], start=True, stop=True))
            rg_midlog.use("T", tv)
            bank_t.writer = tv
        vml = sch.V([("T", bank_t.writer)] + msb_users[0],
                    lambda e: nc.vector.tensor_copy(
            midlog_sb[:, 0:c.GT * c.EW], bank_t.ap[:, 0:c.GT * c.EW]))
        msb_users[0] = []
        bank_t.readers.append(("V", vml))

        bank_gt = psum[4]
        for jl in range(c.GT):
            lg = midlog_sb[:, jl * c.EW + c.NE * c.R:(jl + 1) * c.EW]
            md = midlog_sb[:, jl * c.EW:jl * c.EW + c.NE * c.R]
            m1, m2 = gsc[:, 0:1], gsc[:, 1:2]
            eq, l2 = gsc[:, 4:4 + c.NE], gsc[:, 8:8 + c.NE]
            mask, em = gsc[:, 12:12 + c.NE], gsc[:, 16:16 + c.NE]
            g4v = gsc[:, 20:20 + c.NE]
            p = jl % 2
            c1 = sch.V([("V", vml), ("V", gate_last[0])], lambda e, lg=lg, m1=m1:
                       nc.vector.reduce_max(m1, lg, AX, negate=True))
            c2 = sch.V([("V", c1)], lambda e, lg=lg, m1=m1, eq=eq:
                       nc.vector.tensor_scalar(
                           eq, lg, m1[:, 0:1], 0.0, OP.add, OP.is_equal))
            c3 = sch.V([("V", c2)], lambda e, lg=lg, eq=eq, l2=l2:
                       nc.vector.scalar_tensor_tensor(
                           l2, eq, -1e30, lg, OP.mult, OP.add))
            c4 = sch.V([("V", c3)], lambda e, l2=l2, m2=m2:
                       nc.vector.reduce_max(m2, l2, AX, negate=True))
            vmask = sch.V([("V", c4)], lambda e, lg=lg, m2=m2, mask=mask:
                          nc.vector.tensor_scalar(
                              mask, lg, m2[:, 0:1], 0.0, OP.add, OP.is_ge))
            aex = sch.A([("V", vmask)], lambda e, lg=lg, m1=m1, p=p:
                        nc.scalar.activation(gex[p][:, :], lg, AF.Exp, bias=m1[:, 0:1]))
            c5 = sch.V([("A", aex), ("V", vmask)], lambda e, em=em, p=p, mask=mask:
                       nc.vector.tensor_tensor(em, gex[p][:, :], mask, OP.mult))
            c6 = sch.V([("V", c5)], lambda e, em=em:
                       nc.vector.reduce_sum(gden[:, 0:1], em, AX))
            c7 = sch.V([("V", c6)], lambda e:
                       nc.vector.reciprocal(gden[:, 1:2], gden[:, 0:1]))
            vg = sch.V([("V", c7)], lambda e, em=em, g4v=g4v: nc.vector.tensor_scalar(
                g4v, em, gden[:, 1:2], 4.0, OP.mult, OP.mult))
            wgm = rg_gmid.take(("gm", g, jl))
            vgm = 0
            for e4 in range(c.NE):
                vgm = sch.V([("V", vg)] + (wgm if e4 == 0 else []),
                            lambda e, md=md, e4=e4, g4v=g4v: nc.vector.tensor_scalar(
                                gmid[:, e4 * c.R:(e4 + 1) * c.R],
                                md[:, e4 * c.R:(e4 + 1) * c.R],
                                g4v[:, e4:e4 + 1], 0.0, OP.mult, OP.add))
            w = [("V", vgm), iden_ready] + (bank_gt.readers if jl == 0 else [])
            if jl == 0:
                bank_gt.readers = []
            sch.T(w, lambda e, jl=jl: nc.tensor.matmul(
                bank_gt.ap[0:c.NE * c.R, jl * 128:(jl + 1) * 128],
                gmid[:, :], iden[:, :], start=True, stop=True))
            tv = sch.T([], lambda e, jl=jl: nc.tensor.matmul(
                bank_gt.ap[c.NE * c.R:2 * c.NE * c.R, jl * 128:(jl + 1) * 128],
                gmid[:, :], iden[:, :], start=True, stop=True))
            rg_gmid.use("T", tv)
            bank_gt.writer = tv
        wgt = rg_gmidT.take(("gt", g))
        vgt = sch.V([("T", bank_gt.writer)] + wgt, lambda e: nc.vector.tensor_copy(
            gmidT[:, 0:c.GT * 128], bank_gt.ap[:, 0:c.GT * 128]))
        bank_gt.readers.append(("V", vgt))

        for jl in range(c.GT):
            last_add = 0
            for sl in range(c.D // 512):
                bank = psum[5 + sl % 2]
                w = [("V", vgt), bcat_ready] + bank.readers
                bank.readers = []
                if sl < c.D // 1024:
                    rhs = bcat[0:64, sl * 512:(sl + 1) * 512]
                    lhs = gmidT[0:64, jl * 128:(jl + 1) * 128]
                else:
                    rhs = bcat[64:128, (sl - c.D // 1024) * 512:
                               (sl - c.D // 1024 + 1) * 512]
                    lhs = gmidT[64:128, jl * 128:(jl + 1) * 128]
                tv = sch.T(w, lambda e, b=bank, l=lhs, r=rhs: nc.tensor.matmul(
                    b.ap[:, 0:512], l, r, start=True, stop=True))
                bank.writer = tv
                rg_gmidT.use("T", tv)
                vv = sch.V([("T", tv), ("V", h_add[jl][sl])],
                           lambda e, b=bank, jl=jl, sl=sl:
                           nc.vector.tensor_tensor(
                               h[jl][:, sl * 512:(sl + 1) * 512], b.ap[:, 0:512],
                               h[jl][:, sl * 512:(sl + 1) * 512], OP.add))
                bank.readers.append(("V", vv))
                last_add = vv
                rg_hs[jl].use("V", vv)
            row0 = (g * c.GT + jl) * 128
            xv = sch.X([("V", last_add)], lambda e, r=row0, jl=jl:
                       e.dma_start(out_d[r:r + 128, :], h[jl][:, :]))
            rg_hs[jl].use(*xv)

    sch.raw("sync", [(k, sch.cnt[k]) for k in DMA_FAM["X"]])
    sch.raw("gpsimd", [(k, sch.cnt[k]) for k in DMA_FAM["W"]])
    sch.raw("scalar", [(k, sch.cnt[k]) for k in DMA_FAM["Z"]])

    with nc.Block() as block:
        @block.tensor
        def _(e):
            for f in sch.q["tensor"]:
                f(e)

        @block.scalar
        def _(e):
            for f in sch.q["scalar"]:
                f(e)

        @block.vector
        def _(e):
            for f in sch.q["vector"]:
                f(e)

        @block.sync
        def _(e):
            for f in sch.q["sync"]:
                f(e)

        @block.gpsimd
        def _(e):
            for f in sch.q["gpsimd"]:
                f(e)

    return nc


def make_tables(cfg):
    c = cfg
    ex = np.arange(0, 128, 2, dtype=np.float32) / np.float32(128.0)
    inv = (np.float32(1.0) / (np.float32(10000.0) ** ex)).astype(np.float32)
    t = np.arange(c.S, dtype=np.float32)
    fr = t[:, None] * inv[None, :]
    cosv = np.cos(fr).astype(np.float32)
    sinv = np.sin(fr).astype(np.float32)
    cost = np.concatenate([cosv.T, cosv.T], axis=0)
    sint = np.concatenate([-sinv.T, sinv.T], axis=0)
    iden = np.eye(128, dtype=np.float32).astype(BFNP)
    m = np.arange(128)[:, None]
    q = np.arange(128)[None, :]
    trib = (q >= m).astype(np.float32)
    tri = np.ascontiguousarray(np.concatenate([trib] * 4, axis=1)).astype(BFNP)
    return cost, sint, iden, tri


def make_inputs(cfg, x, wq, wk, wv, wo, A, B, Wr):
    c = cfg
    cost, sint, iden, tri = make_tables(c)
    awr = np.ascontiguousarray(np.concatenate(
        [np.ascontiguousarray(A.transpose(2, 0, 1)).reshape(c.D, c.NE * c.R),
         np.ascontiguousarray(Wr.T)], axis=1), dtype=np.float32)
    bc = np.ascontiguousarray(B.transpose(0, 2, 1)).reshape(c.NE * c.R, c.D)
    bcat = np.ascontiguousarray(
        np.concatenate([bc[:, :c.D // 2], bc[:, c.D // 2:]], axis=0), dtype=np.float32)
    return dict(x=np.ascontiguousarray(x, dtype=np.float32),
                wq=np.ascontiguousarray(wq, dtype=np.float32),
                wk=np.ascontiguousarray(wk, dtype=np.float32),
                wv=np.ascontiguousarray(wv, dtype=np.float32),
                wo=np.ascontiguousarray(wo, dtype=np.float32),
                awr=awr, bcat=bcat, cost=cost, sint=sint, iden=iden, tri=tri)


def kernel(L0_attn_x, L0_attn_Wq, L0_attn_Wk, L0_attn_Wv, L0_attn_Wo,
           L0_ffn_A, L0_ffn_B, L0_ffn_Wr):
    cfg = CFG(D=4096, S=2048, NHQ=32, NHKV=8)
    x = np.asarray(L0_attn_x, dtype=np.float32)
    nb = x.shape[0]
    base = make_inputs(cfg, x[0],
                       np.asarray(L0_attn_Wq), np.asarray(L0_attn_Wk),
                       np.asarray(L0_attn_Wv), np.asarray(L0_attn_Wo),
                       np.asarray(L0_ffn_A), np.asarray(L0_ffn_B),
                       np.asarray(L0_ffn_Wr))
    in_maps = []
    for i in range(nb):
        m = dict(base)
        m["x"] = np.ascontiguousarray(x[i])
        in_maps.append(m)
    nc = build_program(cfg)
    res = run_bass_kernel_spmd(nc, in_maps, core_ids=list(range(nb)))
    out = np.stack([np.asarray(res.results[i]["out"]) for i in range(nb)], axis=0)
    return out.astype(np.float32)
